# version 19
# baseline (speedup 1.0000x reference)
import contextlib
import numpy as np
import ml_dtypes
import concourse.bass as bass
import concourse.mybir as mybir
from concourse.bass_utils import run_bass_kernel_spmd

F32 = mybir.dt.float32
BF16 = mybir.dt.bfloat16
I32 = mybir.dt.int32
AF = mybir.ActivationFunctionType
ALU = mybir.AluOpType

D = 1024
NH = 16
DK = 64
NG = 8
HID = 3072
NHC = HID // 128
NDC = D // 128

ENGMAP = {"pe": "tensor", "act": "scalar", "dve": "vector", "pool": "gpsimd", "sp": "sync"}

DEBUG_BUFS = []
LAST_DEBUG = {}


class Buf:
    def __init__(self):
        self.w = []
        self.r = []

    def rd(self):
        return tuple(self.w)

    def wr(self):
        return tuple(self.w) + tuple(self.r)

    def setw(self, tok):
        self.w = [tok] if tok else []
        self.r = []

    def addw(self, tok):
        if tok:
            self.w.append(tok)

    def addr(self, tok):
        if tok:
            self.r.append(tok)


class Sched:
    def __init__(self, nc, stack):
        self.nc = nc
        self.stack = stack
        self.q = {e: [] for e in ENGMAP}
        self.cnt = {}
        self.know = {e: {} for e in ENGMAP}
        self.esem = {}
        for e in ("pe", "act", "dve", "pool"):
            self.esem[e] = self.sem("sem_" + e)

    def sem(self, name):
        h = self.stack.enter_context(self.nc.semaphore(name))
        self.cnt[id(h)] = 0
        return h

    def emit(self, eng, fn, after=(), signal=None, inc=1):
        own = self.esem.get(eng)
        skip = own if eng == "pe" else None
        waits = {}
        for tok in after:
            if tok is None:
                continue
            sem, val = tok
            if sem is skip:
                continue
            k = id(sem)
            if self.know[eng].get(k, 0) >= val:
                continue
            if k not in waits or waits[k][1] < val:
                waits[k] = (sem, val)
        for k, (sem, val) in waits.items():
            self.know[eng][k] = val
        tok = None
        incs = []
        if signal is not None and signal is not False:
            sems = signal if isinstance(signal, list) else [signal]
            for s_ in sems:
                s_ = own if s_ is True else s_
                self.cnt[id(s_)] += inc
                if tok is None:
                    tok = (s_, self.cnt[id(s_)])
                incs.append((s_, inc))
        self.q[eng].append((fn, list(waits.values()), incs))
        return tok

    def dma(self, eng, sem, out, in_, after=()):
        return self.emit(eng, lambda e, o=out, i=in_: e.dma_start(out=o, in_=i),
                         after=after, signal=sem, inc=16)

    def fence(self):
        return [(s, self.cnt[id(s)]) for s in self.esem.values()]

    def replay(self, block):
        for eng, bassname in ENGMAP.items():
            ops = self.q[eng]

            def body(engine, ops=ops):
                for fn, waits, incs in ops:
                    for sem, val in waits:
                        engine.wait_ge(sem, val)
                    ins = fn(engine)
                    for sem, n in incs:
                        ins.then_inc(sem, n)

            getattr(block, bassname)(body)


def build_program(S):
    nIT = S // 512
    nRT = S // 128
    nJT = S // 128
    CH = min(1024, S)
    NCH = S // CH
    nc = bass.Bass("TRN2", target_bir_lowering=False, debug=False)

    xin = nc.dram_tensor("x", [S, D], F32, kind="ExternalInput").ap()
    wq_l = nc.dram_tensor("wq_l", [NG, NDC, 128, 128], F32, kind="ExternalInput").ap()
    wk_l = nc.dram_tensor("wk_l", [NG, NDC, 128, 128], F32, kind="ExternalInput").ap()
    wv_l = nc.dram_tensor("wv_l", [NG, NDC, 128, 128], F32, kind="ExternalInput").ap()
    wo_l = nc.dram_tensor("wo_l", [NDC, NDC, 128, 128], F32, kind="ExternalInput").ap()
    bq_l = nc.dram_tensor("bq_l", [128, NG], F32, kind="ExternalInput").ap()
    bk_l = nc.dram_tensor("bk_l", [128, NG], F32, kind="ExternalInput").ap()
    bo_l = nc.dram_tensor("bo_l", [128, NDC], F32, kind="ExternalInput").ap()
    bv_l = nc.dram_tensor("bv_l", [1, D], F32, kind="ExternalInput").ap()
    ctab_d = nc.dram_tensor("ctab", [128, S], BF16, kind="ExternalInput").ap()
    stab_d = nc.dram_tensor("stab", [128, S], BF16, kind="ExternalInput").ap()
    tri_d = nc.dram_tensor("tri", [128, 128], BF16, kind="ExternalInput").ap()
    id_d = nc.dram_tensor("ident", [128, 128], BF16, kind="ExternalInput").ap()
    wg_l = nc.dram_tensor("wg_l", [NHC, NDC, 128, 128], I32, kind="ExternalInput").ap()
    wu_l = nc.dram_tensor("wu_l", [NHC, NDC, 128, 128], I32, kind="ExternalInput").ap()
    wd_l = nc.dram_tensor("wd_l", [NHC, 128, D], I32, kind="ExternalInput").ap()
    wsc = nc.dram_tensor("wsc", [3, NHC, 128, D], BF16, kind="Internal").ap()
    out_d = nc.dram_tensor("out", [S, D], F32, kind="ExternalOutput").ap()

    stack = contextlib.ExitStack()
    with stack:
        sb = lambda name, shape, dt: stack.enter_context(nc.sbuf_tensor(name, shape, dt))
        pst = lambda name, shape, dt=F32: stack.enter_context(nc.psum_tensor(name, shape, dt))

        xnT = sb("xnT", [128, max(NDC * S, 32256)], BF16)
        ctxT = sb("ctxT", [128, max(NDC * S, 8192)], BF16)
        SCR = sb("SCR", [128, 28928], BF16)
        ctab = sb("ctab_s", [128, S], BF16)
        stab = sb("stab_s", [128, S], BF16)
        tri = sb("tri_s", [128, 128], BF16)
        idn = sb("id_s", [128, 128], BF16)
        bqs = sb("bqs", [128, NG], F32)
        bks = sb("bks", [128, NG], F32)
        bos = sb("bos", [128, NDC], F32)
        onesr = sb("onesr", [1, 128], BF16)
        lnsc = sb("lnsc", [128, nRT], F32)
        lnbi = sb("lnbi", [128, nRT], F32)
        lnst = sb("lnst", [128, 24], F32)
        lnag = sb("lnag", [128, 4], F32)
        lnsq = sb("lnsq", [128, 2], F32)
        epsv = sb("epsv", [128, 1], F32)
        ceb = sb("ceb", [128, 1024], BF16)
        rct = sb("rct", [128, 2048], BF16)
        rcb2 = [rct.ap()[0:1, 0:512], rct.ap()[0:1, 512:1024]]
        rcbf2 = [rct.ap()[64:65, 0:1024].bitcast(F32), rct.ap()[64:65, 1024:2048].bitcast(F32)]
        bvs = rct.ap()[0:1, 1024:2048]

        Qg = SCR.ap()[:, 0:S]
        Kg = SCR.ap()[:, S:2 * S]
        VT_COLS = nJT * 130
        VT_PAD = 64
        VT_HALF = VT_COLS + VT_PAD
        Vt2 = [SCR.ap()[:, 2 * S + h * VT_HALF: 2 * S + (h + 1) * VT_HALF]
               for h in range(2)]
        base = 2 * S + 2 * VT_HALF
        rsc_a = SCR.ap()[:, base:base + CH]
        rsc_b = SCR.ap()[:, base + 1024:base + 1024 + CH]
        Pslot = [SCR.ap()[:, base + 2048 + i * 3072: base + 2048 + (i + 1) * 3072]
                 for i in range(2)]
        wq_sl = SCR.ap()[:, base + 8192:base + 9216]
        wk_sl = SCR.ap()[:, base + 9216:base + 10240]
        wv_sl = SCR.ap()[:, base + 10240:base + 12288]
        assert base + 12288 <= 28928, "SCR overflow (P1)"

        xsl = [ctxT.ap()[:, i * 2048:(i + 1) * 2048].bitcast(F32) for i in range(3)]
        xnb0 = [ctxT.ap()[:, 6144 + i * 1024: 6144 + (i + 1) * 1024] for i in range(2)]

        wo_bf = SCR.ap()[:, 0:8192]
        wg_sl = [SCR.ap()[:, 8192 + i * 1024:8192 + (i + 1) * 1024] for i in range(2)]
        wu_sl = [SCR.ap()[:, 10240 + i * 1024:10240 + (i + 1) * 1024] for i in range(2)]
        wdA_sl = [SCR.ap()[:, 12288 + i * 512:12288 + (i + 1) * 512] for i in range(2)]
        wdB_sl = [SCR.ap()[:, 13312 + i * 512:13312 + (i + 1) * 512] for i in range(2)]
        xp3 = [SCR.ap()[:, 14336 + i * 2048:14336 + (i + 1) * 2048].bitcast(F32)
               for i in range(4)]
        obuf = [SCR.ap()[:, 22528 + i * 2048:22528 + (i + 1) * 2048].bitcast(F32)
                for i in range(2)]
        hpr = xnT.ap()[:, 0:12288]
        s_t = xnT.ap()[:, 12288:16384]
        prod = xnT.ap()[:, 16384:20480]
        xnt_t = xnT.ap()[:, 20480:24576]
        sgb = xnT.ap()[:, 24576:25088]
        gbf = xnT.ap()[:, 25088:27136]
        xnb3 = xnT.ap()[:, 27136:28160]

        S6 = pst("S6", [128, 3072], F32)
        C0 = pst("C0", [128, 512], F32)
        C1 = pst("C1", [128, 512], F32)
        S6_bf = S6.ap().bitcast(BF16)
        C0_bf = C0.ap().bitcast(BF16)

        sch = Sched(nc, stack)
        xq_sl = [sch.sem(f"xq{i}") for i in range(3)]
        xp_sl = [sch.sem(f"xp{i}") for i in range(6)]
        wqs_s = sch.sem("wqs")
        wks_s = sch.sem("wks")
        wvs_s = sch.sem("wvs")
        wgs_s = [sch.sem(f"wgs{i}") for i in range(2)]
        wus_s = [sch.sem(f"wus{i}") for i in range(2)]
        wdAs_s = [sch.sem(f"wdA{i}") for i in range(2)]
        wdBs_s = [sch.sem(f"wdB{i}") for i in range(2)]
        sw_s = sch.sem("sw")
        aw_s = sch.sem("aw")
        cs_s = sch.sem("cs")
        cv_s = sch.sem("cv")
        sc_s = sch.sem("scw")
        fw_s = sch.sem("fw")
        fd_s = sch.sem("fd")
        out_s = sch.sem("outs")

        B = Buf
        b_xnT, b_ctxT = B(), B()
        b_Qg, b_Kg = B(), B()
        b_Vt2 = [B(), B()]
        b_rsa, b_rsb, b_t1 = B(), B(), B()
        b_P = [B(), B()]
        b_wq, b_wk, b_wv = B(), B(), B()
        b_xsl = [B(), B(), B()]
        b_xnb0 = [B(), B()]

        b_ln = B()
        b_lnag = [B(), B()]
        b_lnsq = [B(), B()]
        b_ceb = [B(), B()]
        b_rcs = [B(), B()]
        b_S6 = [B() for _ in range(6)]
        b_SAb = b_S6[0:3]
        b_SBb = b_S6[3:6]
        b_C = [B(), B()]
        b_wo = B()
        b_wg = [B(), B()]
        b_wu = [B(), B()]
        b_wdA = [B(), B()]
        b_wdB = [B(), B()]
        b_xp3 = [B(), B(), B(), B()]
        b_ob = [B(), B()]
        b_hpr, b_st, b_prod, b_xnt3, b_sgb, b_gbf, b_xnb3 = B(), B(), B(), B(), B(), B(), B()

        E = sch.emit
        DMA = sch.dma

        DMA("pool", cs_s, ctab.ap()[:, :], ctab_d)
        DMA("pool", cs_s, stab.ap()[:, :], stab_d)
        DMA("pool", cs_s, tri.ap()[:, :], tri_d)
        DMA("pool", cs_s, idn.ap()[:, :], id_d)
        DMA("pool", cs_s, bqs.ap()[:, :], bq_l)
        DMA("pool", cs_s, bks.ap()[:, :], bk_l)
        DMA("pool", cs_s, bos.ap()[:, :], bo_l)
        DMA("pool", cs_s, bvs, bv_l)
        cs_tok = (cs_s, 16 * 8)

        E("pool", lambda g: g.memset(onesr.ap()[:, :], 1.0), signal=True)
        eps_tok = E("pool", lambda g: g.memset(epsv.ap()[:, :], 1e-5), signal=True)
        vt3h = [Vt2[h][:, 0:VT_COLS].rearrange("p (t c) -> p t c", t=nJT, c=130)
                for h in range(2)]
        E("pool", lambda g: g.memset(Vt2[0][:, VT_COLS:VT_HALF], 0.0), signal=True)
        E("pool", lambda g: g.memset(Vt2[1][:, VT_COLS:VT_HALF], 0.0), signal=True)
        E("pool", lambda g: g.memset(vt3h[0][:, :, 64:65], 1.0), signal=True)
        E("pool", lambda g: g.memset(vt3h[0][:, :, 129:130], 1.0), signal=True)
        E("pool", lambda g: g.memset(vt3h[1][:, :, 64:65], 1.0), signal=True)
        mem_tok = E("pool", lambda g: g.memset(vt3h[1][:, :, 129:130], 1.0), signal=True)
        b_Vt2[0].addw(mem_tok)
        b_Vt2[1].addw(mem_tok)


        rr = lambda ap: ap.rearrange("a p c -> p a c")
        t = DMA("pool", wqs_s, wq_sl.rearrange("p (a c) -> p a c", a=NDC, c=128), rr(wq_l[0]))
        b_wq.setw(t)
        t = DMA("pool", wks_s, wk_sl.rearrange("p (a c) -> p a c", a=NDC, c=128), rr(wk_l[0]))
        b_wk.setw(t)
        wv4 = wv_sl.rearrange("p (a b c) -> p a b c", a=NDC, b=2, c=128)
        t = DMA("pool", wvs_s, wv4[:, :, 0, :], rr(wv_l[0]))
        b_wv.setw(t)
        t = DMA("pool", wvs_s, wv4[:, :, 1, :], rr(wv_l[1]))
        b_wv.addw(t)

        p0ps = [S6_bf[:, 0:1024], S6_bf[:, 1024:2048]]
        b_p0ps = [b_SAb[0], b_SAb[1]]
        xnT4 = xnT.ap()[:, 0:NDC * S].rearrange("p (a r c) -> p a r c", a=NDC, r=nRT, c=128)

        for rt in range(nRT):
            sl = rt % 3
            sl2 = rt % 2
            q0 = rt % 2
            lnst_s = lnst.ap()[:, q0 * 12:(q0 + 1) * 12]
            lnag_s = lnag.ap()[:, q0 * 2:(q0 + 1) * 2]
            lnsq_s = lnsq.ap()[:, q0:q0 + 1]
            b_lnag_s = b_lnag[q0]
            b_lnsq_s = b_lnsq[q0]
            xt = DMA("sp", xq_sl[sl], xsl[sl][:, :], xin[rt * 128:(rt + 1) * 128, :],
                     after=b_xsl[sl].wr())
            b_xsl[sl].setw(xt)
            ts1 = E("dve", lambda v, s=sl, L=lnst_s: v.bn_stats(L[:, 0:6], xsl[s][:, 0:512]),
                    after=b_xsl[sl].rd(), signal=True)
            ts2 = E("dve", lambda v, s=sl, L=lnst_s: v.bn_stats(L[:, 6:12], xsl[s][:, 512:1024]),
                    after=b_xsl[sl].rd(), signal=True)
            agt = E("dve", lambda v, L=lnst_s, A=lnag_s: v.bn_aggr(A[:, :], L[:, :]),
                    after=(ts1, ts2) + b_lnag_s.wr(), signal=True)
            b_lnag_s.setw(agt)
            sqt = E("act", lambda sc, A=lnag_s, Q=lnsq_s: sc.activation(
                Q[:, :], A[:, 1:2], AF.Sqrt, bias=epsv.ap()[:, :]),
                    after=(agt, cs_tok, eps_tok) + b_lnsq_s.wr(), signal=True)
            b_lnag_s.addr(sqt)
            b_lnsq_s.setw(sqt)
            rct2 = E("dve", lambda v, r=rt, Q=lnsq_s: v.reciprocal(lnsc.ap()[:, r:r + 1], Q[:, :]),
                     after=(sqt,), signal=True)
            b_lnsq_s.addr(rct2)
            stt = E("dve", lambda v, r=rt, A=lnag_s: v.scalar_tensor_tensor(
                lnbi.ap()[:, r:r + 1], A[:, 0:1], -1.0,
                lnsc.ap()[:, r:r + 1], ALU.mult, ALU.mult),
                    after=(rct2,), signal=True)
            b_lnag_s.addr(stt)
            b_ln.addw(stt)
            xnbt = E("act", lambda sc, s=sl, s2=sl2, r=rt: sc.activation(
                xnb0[s2][:, :], xsl[s][:, :], AF.Identity,
                bias=lnbi.ap()[:, r:r + 1], scale=lnsc.ap()[:, r:r + 1]),
                after=(stt,) + b_xsl[sl].rd() + b_xnb0[sl2].wr(), signal=True)
            b_xsl[sl].addr(xnbt)
            b_xnb0[sl2].setw(xnbt)
            ptok = None
            for dc in range(NDC):
                ptok = E("pe", lambda te, s2=sl2, d=dc: te.transpose(
                    p0ps[s2][:, d * 128:(d + 1) * 128],
                    xnb0[s2][:, d * 128:(d + 1) * 128], idn.ap()[:, :]),
                    after=(cs_tok, xnbt) + (b_p0ps[sl2].wr() if dc == 0 else ()),
                    signal=True)
            b_xnb0[sl2].addr(ptok)
            b_p0ps[sl2].setw(ptok)
            psv = p0ps[sl2].rearrange("p (a c) -> p a c", a=NDC, c=128)
            ctok = E("dve", lambda v, r=rt, pv=psv: v.tensor_copy(
                xnT4[:, :, r, :], pv[:, :, :]),
                after=(ptok,), signal=True)
            b_p0ps[sl2].addr(ctok)
            b_xnT.addw(ctok)

        ndbg_e = [0]
        dbg_extra = []
        def early_dump(nm, ap, shp, dt):
            if nm in DEBUG_BUFS:
                dd = nc.dram_tensor("dbg_" + nm, shp, dt, kind="ExternalOutput").ap()
                tok = DMA("sp", out_s, dd, ap, after=sch.fence())
                ndbg_e[0] += 1
                dbg_extra.append(tok)
                return tok
            return None
        t0 = early_dump("xnT0", xnT.ap()[:, 0:NDC * S], [128, NDC * S], BF16)
        if t0 is not None:
            b_xnT.addr(t0)
        for t in sch.fence():
            b_ctxT.addr(t)

        def qkv_qk(g, which):
            w_ap = wq_sl if which == "q" else wk_sl
            b_w = b_wq if which == "q" else b_wk
            dst = Qg if which == "q" else Kg
            b_dst = b_Qg if which == "q" else b_Kg
            bias = bqs if which == "q" else bks
            for half in range(NCH):
                etoks = []
                for k4 in range(CH // 512):
                    it = half * (CH // 512) + k4
                    reg = S6.ap()[:, (k4 % 2) * 512:(k4 % 2) * 512 + 512]
                    bank = b_SAb[k4 % 2]
                    mmt = None
                    for dc in range(NDC):
                        mmt = E("pe", lambda te, i=it, d=dc, r=reg, w=w_ap: te.matmul(
                            r[:, :], w[:, d * 128:(d + 1) * 128],
                            xnT.ap()[:, d * S + i * 512: d * S + (i + 1) * 512],
                            start=(d == 0), stop=(d == NDC - 1), skip_group_check=True),
                            after=b_xnT.rd() + b_w.rd() + (bank.wr() if dc == 0 else ()),
                            signal=(dc == NDC - 1))
                    bank.setw(mmt)
                    b_w.addr(mmt)
                    et = E("dve", lambda v, k_=k4, r=reg, g_=g, b_=bias: v.tensor_scalar(
                        rsc_a[:, k_ * 512:(k_ + 1) * 512], r[:, :],
                        b_.ap()[:, g_:g_ + 1], None, ALU.add),
                        after=(mmt, cs_tok) + b_rsa.wr(), signal=True)
                    bank.addr(et)
                    etoks.append(et)
                b_rsa.setw(etoks[0])
                for e2 in etoks[1:]:
                    b_rsa.addw(e2)
                stoks = []
                for (d0, s0) in ((0, 32), (32, 0), (64, 96), (96, 64)):
                    st = DMA("sp", sw_s, rsc_b[d0:d0 + 32, :], rsc_a[s0:s0 + 32, :],
                             after=b_rsa.rd() + b_rsb.wr())
                    stoks.append(st)
                b_rsb.setw(stoks[0])
                for s2 in stoks[1:]:
                    b_rsb.addw(s2)
                for s2 in stoks:
                    b_rsa.addr(s2)
                co = half * CH
                t1 = E("dve", lambda v, c=co, d_=dst: v.tensor_mul(
                    d_[:, c:c + CH], ctab.ap()[:, c:c + CH], rsc_a[:, :]),
                    after=(cs_tok,) + b_rsa.rd() + b_dst.wr(), signal=True)
                b_rsa.addr(t1)
                t2 = E("pool", lambda p, c=co: p.tensor_mul(
                    rsc_b[:, :], stab.ap()[:, c:c + CH], rsc_b[:, :]),
                    after=(cs_tok,) + b_rsb.wr(), signal=True)
                b_rsb.setw(t2)
                t3 = E("dve", lambda v, c=co, d_=dst: v.tensor_add(
                    d_[:, c:c + CH], d_[:, c:c + CH], rsc_b[:, :]),
                    after=(t1, t2) + b_dst.wr(), signal=True)
                b_rsb.addr(t3)
                b_dst.addw(t3)

        def qkv_v(g):
            for jt in range(nJT):
                reg = (C0 if jt % 2 == 0 else C1).ap()[:, 0:256]
                bank = b_C[jt % 2]
                for dc in range(NDC):
                    E("pe", lambda te, j=jt, d=dc, r=reg: te.matmul(
                        r[:, :], xnT.ap()[:, d * S + j * 128: d * S + j * 128 + 128],
                        wv_sl[:, d * 256:(d + 1) * 256],
                        start=(d == 0), stop=False, skip_group_check=True),
                        after=b_xnT.rd() + b_wv.rd() + (bank.wr() if dc == 0 else ()),
                        signal=False)
                mmt = E("pe", lambda te, r=reg, g_=g: te.matmul(
                    r[:, :], onesr.ap()[:, :], bvs[:, g_ * 128:(g_ + 2) * 128],
                    start=False, stop=True, skip_group_check=True),
                    after=(mem_tok, cs_tok), signal=True)
                bank.setw(mmt)
                b_wv.addr(mmt)
                dst0 = Vt2[0][:, jt * 130: jt * 130 + 130].rearrange(
                    "p (b c) -> p b c", b=2, c=65)
                dst1 = Vt2[1][:, jt * 130: jt * 130 + 130].rearrange(
                    "p (b c) -> p b c", b=2, c=65)
                srcv = reg.rearrange("p (h b c) -> p h b c", h=2, b=2, c=64)
                et = E("act", lambda sc, d_=dst0, s_=srcv: sc.activation(
                    d_[:, :, 0:64], s_[:, 0, :, :], AF.Copy),
                    after=(mmt,) + b_Vt2[0].wr(), signal=True)
                bank.addr(et)
                b_Vt2[0].addw(et)
                et = E("act", lambda sc, d_=dst1, s_=srcv: sc.activation(
                    d_[:, :, 0:64], s_[:, 1, :, :], AF.Copy),
                    after=(mmt,) + b_Vt2[1].wr(), signal=True)
                bank.addr(et)
                b_Vt2[1].addw(et)

        sgrp = [0]
        cpar = [0]
        fin_i = [0]

        fin_pending = []

        def flush_fin():
            while fin_pending:
                fin_pending.pop(0)()

        def emit_finalize(Cc, b_Cc, last_ctx_tok, g, h2, it):
            cslot = fin_i[0] % 2
            fin_i[0] += 1
            ev = E("dve", lambda v, C_=Cc, s=cslot: v.tensor_copy(
                ceb.ap()[:, s * 512:(s + 1) * 512][0:64, :], C_.ap()[0:64, :]),
                after=(last_ctx_tok,) + b_ceb[cslot].wr(), signal=True)
            b_Cc.addr(ev)
            b_ceb[cslot].setw(ev)
            rc = E("act", lambda sc, C_=Cc, s=cslot: sc.activation(
                rcbf2[s][:, :], C_.ap()[64:65, :], AF.Ln),
                   after=(ev,) + b_rcs[cslot].wr(), signal=True)
            b_Cc.addr(rc)
            rc = E("act", lambda sc, s=cslot: sc.activation(
                rcb2[s][:, :], rcbf2[s][:, :], AF.Exp, scale=-1.0),
                   after=(rc,), signal=True)
            b_rcs[cslot].setw(rc)

            def doit():
                bc = E("pe", lambda te, C_=Cc, s=cslot: te.matmul(
                    C_.ap()[0:64, 0:512], onesr.ap()[0:1, 0:64], rcb2[s][:, :],
                    start=True, stop=True, skip_group_check=True),
                    after=(rc,) + b_Cc.wr(), signal=True)
                b_Cc.setw(bc)
                b_rcs[cslot].addr(bc)
                cn = E("dve", lambda v, C_=Cc, s=cslot, g_=g, h_=h2, i_=it: v.tensor_mul(
                    ctxT.ap()[64 * h_:64 * h_ + 64,
                              g_ * S + i_ * 512: g_ * S + (i_ + 1) * 512],
                    ceb.ap()[:, s * 512:(s + 1) * 512][0:64, :], C_.ap()[0:64, 0:512]),
                    after=(ev, bc) + b_ctxT.wr(), signal=True)
                b_Cc.addr(cn)
                b_ceb[cslot].addr(cn)
                b_ctxT.addw(cn)

            fin_pending.append(doit)

        def attention(g):
            b_Vt = b_Vt2[g % 2]
            Vt = Vt2[g % 2]
            """Both heads of group g, h1 staggered one score-group behind h0.

            PSUM: S6 banks 0-2 = h0 scores, banks 3-5 = h1 scores; C0/C1 = ctx.
            P slots: per step parity, h0 at [0:1536], h1 at [1536:3072]."""
            for it in range(nIT):
                nfull = 4 * it
                groups = []
                j0 = 0
                while j0 < nfull:
                    n = min(3, nfull - j0)
                    groups.append([(j0 + k, 512, 0, k * 512) for k in range(n)])
                    j0 += n
                groups.append([(4 * it + 0, 512, 0, 0),
                               (4 * it + 1, 384, 128, 512),
                               (4 * it + 3, 128, 384, 896),
                               (4 * it + 2, 256, 256, 1024)])
                total = len(groups)
                ndiag = total - 1
                first_ctx = [True, True]
                last_ctx_tok = [None, None]
                pend = {}

                def step(k):
                    specs = []
                    for h2, gk in ((0, k), (1, k - 1)):
                        if not (0 <= gk < total):
                            continue
                        grp = groups[gk]
                        is_diag = (gk == ndiag)
                        Sreg = S6.ap()[:, h2 * 1536:(h2 + 1) * 1536]
                        b_Sb = b_S6[h2 * 3:(h2 + 1) * 3]
                        klo = 64 * h2
                        wtot = sum(w for (_, w, _, _) in grp)
                        dep0 = (b_Qg.rd() + b_Kg.rd() + b_Sb[0].wr() + b_Sb[1].wr()
                                + b_Sb[2].wr())
                        specs.append([h2, gk, grp, Sreg, b_Sb, klo, dep0, wtot,
                                      is_diag, None])
                    par = k % 2
                    Pb = Pslot[par]
                    b_Pb = b_P[par]
                    maxlen = max((len(s[2]) for s in specs), default=0)
                    for i in range(maxlen):
                        for sp_ in specs:
                            h2, gk, grp, Sreg, b_Sb, klo, dep0, wtot, is_diag, _ = sp_
                            if i >= len(grp):
                                continue
                            jt, w, r, off = grp[i]
                            tok = E("pe", lambda te, j=jt, w_=w, r_=r, o=off,
                                    S_=Sreg, i_=it, kl=klo: te.matmul(
                                S_[0:128, o:o + w_],
                                Kg[kl:kl + 64, j * 128:(j + 1) * 128],
                                Qg[kl:kl + 64, i_ * 512 + r_: i_ * 512 + r_ + w_],
                                start=True, stop=True, skip_group_check=True,
                                tile_position=(kl, 0)),
                                after=(dep0 if i == 0 else ()),
                                signal=(i == len(grp) - 1))
                            if i == len(grp) - 1:
                                sp_[9] = tok
                    for sp_ in specs:
                        h2, gk, grp, Sreg, b_Sb, klo, dep0, wtot, is_diag, stok = sp_
                        for bb in b_Sb:
                            bb.setw(stok)
                        b_Qg.addr(stok)
                        b_Kg.addr(stok)
                    merged = (len(specs) == 2 and specs[0][7] == 1536
                              and specs[1][7] == 1536 and not specs[0][8]
                              and not specs[1][8])
                    if merged:
                        etok = E("act", lambda sc, P_=Pb: sc.activation(
                            P_[:, 0:3072], S6.ap()[:, 0:3072], AF.Exp, scale=0.125),
                            after=(specs[0][9], specs[1][9]) + b_Pb.wr(), signal=True)
                        for bb in b_S6:
                            bb.addr(etok)
                        b_Pb.setw(etok)
                        etoks = {0: etok, 1: etok}
                        mtoks = {0: etok, 1: etok}
                    else:
                        etoks = {}
                        mtoks = {}
                        for sp_ in specs:
                            h2, gk, grp, Sreg, b_Sb, klo, dep0, wtot, is_diag, stok = sp_
                            wexp = 1280 if is_diag else wtot
                            etok = E("act", lambda sc, w_=wexp, h_=h2, P_=Pb: sc.activation(
                                P_[:, h_ * 1536: h_ * 1536 + w_],
                                S6.ap()[:, h_ * 1536: h_ * 1536 + w_],
                                AF.Exp, scale=0.125),
                                after=(stok,) + b_Pb.wr(), signal=True)
                            for bb in b_Sb:
                                bb.addr(etok)
                            b_Pb.addw(etok)
                            etoks[h2] = etok
                            mtoks[h2] = etok
                    for sp_ in specs:
                        h2, gk, grp, Sreg, b_Sb, klo, dep0, wtot, is_diag, stok = sp_
                        if is_diag:
                            mtok = etoks[h2]
                            for kk, (jt, w, r, off) in enumerate(grp):
                                mtok = E("dve", lambda v, o=off, h_=h2, P_=Pb: v.tensor_mul(
                                    P_[:, h_ * 1536 + o: h_ * 1536 + o + 128],
                                    P_[:, h_ * 1536 + o: h_ * 1536 + o + 128],
                                    tri.ap()[:, :]),
                                    after=(etoks[h2], cs_tok), signal=(kk == len(grp) - 1))
                            b_Pb.addw(mtok)
                            mtoks[h2] = mtok
                        pend[(h2, gk)] = (grp, par, mtoks[h2], is_diag)
                    if k == 1:
                        flush_fin()
                    for h2, gk in ((0, k - 1), (1, k - 2)):
                        if (h2, gk) not in pend:
                            continue
                        grp, ppar, mtok, is_diag = pend.pop((h2, gk))
                        Pb2 = Pslot[ppar]
                        b_Pb2 = b_P[ppar]
                        Cc = C0 if h2 == 0 else C1
                        b_Cc = b_C[h2]
                        vo = 65 * h2
                        for kk, (jt, w, r, off) in enumerate(grp):
                            dep = (mtok,) + b_Vt.rd()
                            if first_ctx[h2]:
                                dep = dep + b_Cc.wr()
                            last_ctx_tok[h2] = E("pe", lambda te, j=jt, w_=w, r_=r,
                                                 o=off, P_=Pb2, C_=Cc, v_=vo, h_=h2,
                                                 st=first_ctx[h2],
                                                 sp=(is_diag and kk == len(grp) - 1): te.matmul(
                                C_.ap()[0:128, r_:r_ + w_],
                                Vt[:, j * 130 + v_: j * 130 + v_ + 128],
                                P_[:, h_ * 1536 + o: h_ * 1536 + o + w_],
                                start=st, stop=sp, skip_group_check=True),
                                after=dep,
                                signal=(is_diag and kk == len(grp) - 1))
                            first_ctx[h2] = False
                        b_Pb2.addr(last_ctx_tok[h2])
                        b_Vt.addr(last_ctx_tok[h2])
                        if is_diag:
                            b_C[h2].setw(last_ctx_tok[h2])
                            emit_finalize(Cc, b_C[h2], last_ctx_tok[h2], g, h2, it)

                for k in range(total + 2):
                    step(k)

        conv_list = [(t, hc) for t in range(3) for hc in range(NHC)]
        conv_i = [0]

        def emit_conversions(n):
            for _ in range(n):
                if conv_i[0] >= len(conv_list):
                    return
                tt, hc = conv_list[conv_i[0]]
                conv_i[0] += 1
                sl = conv_i[0] % 2
                if tt == 0:
                    srcp = wg_l[hc].rearrange("a p c -> p a c")
                elif tt == 1:
                    srcp = wu_l[hc].rearrange("a p c -> p a c")
                else:
                    srcp = wd_l[hc]
                dstp = (wsc[tt, hc].rearrange("p (a c) -> p a c", a=NDC, c=128)
                        if tt < 2 else wsc[tt, hc])
                DMA("pool", sc_s, dstp, srcp)

        for g in range(NG):
            qkv_qk(g, "q")
            qkv_qk(g, "k")
            if g % 2 == 0:
                qkv_v(g)
            if g == 0:
                tq0 = early_dump("Qg0", Qg, [128, S], BF16)
                if tq0 is not None:
                    b_Qg.addr(tq0)
                tk0 = early_dump("Kg0", Kg, [128, S], BF16)
                if tk0 is not None:
                    b_Kg.addr(tk0)
                tv0 = early_dump("Vt0", Vt2[0][:, 0:VT_COLS], [128, VT_COLS], BF16)
                if tv0 is not None:
                    b_Vt2[0].addr(tv0)
            ptoks = None
            if g + 1 < NG:
                tq = DMA("pool", wqs_s, wq_sl.rearrange("p (a c) -> p a c", a=NDC, c=128),
                         rr(wq_l[g + 1]), after=b_wq.wr())
                tk = DMA("pool", wks_s, wk_sl.rearrange("p (a c) -> p a c", a=NDC, c=128),
                         rr(wk_l[g + 1]), after=b_wk.wr())
                ptoks = [tq, tk]
                if g % 2 == 1:
                    tv = DMA("pool", wvs_s, wv4[:, :, 0, :], rr(wv_l[g + 1]),
                             after=b_wv.wr())
                    tv2 = DMA("pool", wvs_s, wv4[:, :, 1, :], rr(wv_l[g + 2]),
                              after=b_wv.wr())
                    ptoks += [tv, tv2]
            emit_conversions(9 if g < NG - 1 else 100)
            attention(g)
            if ptoks:
                b_wq.setw(ptoks[0])
                b_wk.setw(ptoks[1])
                if len(ptoks) > 2:
                    b_wv.setw(ptoks[2])
                    b_wv.addw(ptoks[3])

        flush_fin()
        fence = sch.fence()
        sc_tok = (sc_s, 16 * len(conv_list))
        fence.append(sc_tok)

        wot = DMA("pool", aw_s, wo_bf.rearrange("p (a b c) -> p a b c", a=NDC, b=NDC, c=128),
                  wo_l.rearrange("a b p c -> p a b c"), after=fence)
        b_wo.setw(wot)

        dn_ap = [S6.ap()[:, 1536:2048], S6.ap()[:, 2048:2560], S6.ap()[:, 2560:3072],
                 C0.ap()[:, 0:512]]
        dn_b = [b_S6[3], b_S6[4], b_S6[5], b_C[0]]
        tp_ap = [S6_bf[:, 3072:4096], S6_bf[:, 4096:5120], S6_bf[:, 5120:6144],
                 C0_bf[:, 0:1024]]

        def loadA(hc):
            sl = hc % 2
            t = DMA("act", wgs_s[sl], wg_sl[sl], wsc[0, hc], after=b_wg[sl].wr() + tuple(fence))
            b_wg[sl].setw(t)
            t = DMA("act", wus_s[sl], wu_sl[sl], wsc[1, hc], after=b_wu[sl].wr() + tuple(fence))
            b_wu[sl].setw(t)
            t = DMA("act", wdAs_s[sl], wdA_sl[sl], wsc[2, hc][:, 0:512],
                    after=b_wdA[sl].wr() + tuple(fence))
            b_wdA[sl].setw(t)

        def loadB(hc):
            sl = hc % 2
            t = DMA("act", wdBs_s[sl], wdB_sl[sl], wsc[2, hc][:, 512:1024],
                    after=b_wdB[sl].wr() + tuple(fence))
            b_wdB[sl].setw(t)

        loadA(0)
        loadA(1)
        loadB(0)
        loadB(1)

        xnt_t2 = [xnt_t, xnT.ap()[:, 28160:32256]]
        b_xnt2 = [b_xnt3, Buf()]
        xp6 = [SCR.ap()[:, 14336 + i * 2048:14336 + (i + 1) * 2048].bitcast(F32)
               for i in range(6)]
        b_xp6 = [Buf() for _ in range(6)]
        lnps = S6_bf[:, 2048:3072]
        b_lnps = b_S6[2]

        def emit_ln(it, rps=(0, 1, 2, 3)):
            dst = xnt_t2[it % 2]
            b_dst = b_xnt2[it % 2]
            dv = dst.rearrange("p (a r c) -> p a r c", a=NDC, r=4, c=128)
            for rp in rps:
                rt = it * 4 + rp
                slot = rt % 6
                xt = DMA("sp", xp_sl[slot], xp6[slot][:, :], xin[rt * 128:(rt + 1) * 128, :],
                         after=b_xp6[slot].wr() + tuple(fence))
                b_xp6[slot].setw(xt)
                at = E("act", lambda sc, r=rt, s=slot: sc.activation(
                    xnb3[:, :], xp6[s][:, :], AF.Identity,
                    bias=lnbi.ap()[:, r:r + 1], scale=lnsc.ap()[:, r:r + 1]),
                    after=b_xp6[slot].rd() + b_ln.rd() + b_xnb3.wr() + tuple(fence),
                    signal=True)
                b_xp6[slot].addr(at)
                b_xnb3.setw(at)
                tt = None
                for dc in range(NDC):
                    tt = E("pe", lambda te, d=dc: te.transpose(
                        lnps[:, d * 128:(d + 1) * 128], xnb3[:, d * 128:(d + 1) * 128],
                        idn.ap()[:, :]),
                        after=(at,) + (b_lnps.wr() if dc == 0 else ()),
                        signal=True)
                b_xnb3.addr(tt)
                b_lnps.setw(tt)
                lv = lnps.rearrange("p (a c) -> p a c", a=NDC, c=128)
                ct2 = E("dve", lambda v, p=rp, lv_=lv, dv_=dv: v.tensor_copy(
                    dv_[:, :, p, :], lv_[:, :, :]),
                    after=(tt,) + b_dst.wr(), signal=True)
                b_lnps.addr(ct2)
                b_dst.addw(ct2)

        def emit_stile(it):
            for oc in range(NDC):
                reg = S6.ap()[:, (oc % 2) * 512:(oc % 2) * 512 + 512]
                bank = b_S6[oc % 2]
                mt = None
                for dc in range(NDC):
                    mt = E("pe", lambda te, o=oc, d=dc, r=reg, i=it: te.matmul(
                        r[:, :], wo_bf[:, o * 1024 + d * 128: o * 1024 + (d + 1) * 128],
                        ctxT.ap()[:, d * S + i * 512: d * S + (i + 1) * 512],
                        start=(d == 0), stop=(d == NDC - 1), skip_group_check=True),
                        after=b_wo.rd() + b_ctxT.rd() + (bank.wr() if dc == 0 else ())
                        + tuple(fence),
                        signal=(dc == NDC - 1))
                bank.setw(mt)
                et = E("act", lambda sc, o=oc, r=reg: sc.activation(
                    s_t[:, o * 512:(o + 1) * 512], r[:, :], AF.Identity,
                    bias=bos.ap()[:, o:o + 1]),
                    after=(mt, cs_tok) + b_st.wr() + tuple(fence), signal=True)
                bank.addr(et)
                b_st.addw(et)

        emit_stile(0)
        emit_ln(0)

        for it in range(nIT):
            xnt_c = xnt_t2[it % 2]
            b_xntc = b_xnt2[it % 2]
            gateb = S6.ap()[:, 0:512]
            upb = S6.ap()[:, 512:1024]
            dn_pend = None

            def emit_downA(job):
                hcj, slj, hpj = job
                for oc in range(4):
                    dt = E("pe", lambda te, h=hcj, o=oc, s=slj: te.matmul(
                        dn_ap[o][:, :], wdA_sl[s][:, o * 128:(o + 1) * 128],
                        hpr[:, h * 512:(h + 1) * 512],
                        start=(h == 0), stop=(h == NHC - 1), skip_group_check=True),
                        after=(hpj,) + b_wdA[slj].rd()
                        + (dn_b[oc].wr() if hcj == 0 else ()),
                        signal=(oc == 3))
                    if hcj == 0 and oc == 3:
                        for o2 in range(4):
                            dn_b[o2].setw(dt)
                    elif hcj == NHC - 1 and oc == 3:
                        for o2 in range(4):
                            dn_b[o2].addw(dt)
                    b_wdA[slj].addr(dt)

            for hc in range(NHC):
                sl = hc % 2
                gt = None
                for dc in range(NDC):
                    gt = E("pe", lambda te, d=dc, s=sl, x_=xnt_c: te.matmul(
                        gateb[:, :], wg_sl[s][:, d * 128:(d + 1) * 128],
                        x_[:, d * 512:(d + 1) * 512],
                        start=(d == 0), stop=(d == NDC - 1), skip_group_check=True),
                        after=b_wg[sl].rd() + b_xntc.rd()
                        + (b_S6[0].wr() if dc == 0 else ()),
                        signal=(dc == NDC - 1))
                b_S6[0].setw(gt)
                b_wg[sl].addr(gt)
                b_xntc.addr(gt)
                ut = None
                for dc in range(NDC):
                    ut = E("pe", lambda te, d=dc, s=sl, x_=xnt_c: te.matmul(
                        upb[:, :], wu_sl[s][:, d * 128:(d + 1) * 128],
                        x_[:, d * 512:(d + 1) * 512],
                        start=(d == 0), stop=(d == NDC - 1), skip_group_check=True),
                        after=b_wu[sl].rd() + b_xntc.rd()
                        + (b_S6[1].wr() if dc == 0 else ()),
                        signal=(dc == NDC - 1))
                b_S6[1].setw(ut)
                b_wu[sl].addr(ut)
                b_xntc.addr(ut)
                sg = E("act", lambda sc: sc.activation(
                    sgb[:, :], gateb[:, :], AF.Silu, scale=1.0 / 127.0),
                    after=(gt,) + b_sgb.wr(), signal=True)
                b_S6[0].addr(sg)
                b_sgb.setw(sg)
                hp = E("dve", lambda v, h=hc: v.tensor_mul(
                    hpr[:, h * 512:(h + 1) * 512], sgb[:, :], upb[:, :]),
                    after=(sg, ut) + b_hpr.wr(), signal=True)
                b_S6[1].addr(hp)
                b_sgb.addr(hp)
                b_hpr.addw(hp)
                if dn_pend is not None:
                    emit_downA(dn_pend)
                    nhc = dn_pend[0] + 2
                    if nhc < NHC:
                        slw = nhc % 2
                        t = DMA("act", wdAs_s[slw], wdA_sl[slw], wsc[2, nhc][:, 0:512],
                                after=b_wdA[slw].wr() + tuple(fence))
                        b_wdA[slw].setw(t)
                dn_pend = (hc, sl, hp)
                if hc + 2 < NHC:
                    slw = (hc + 2) % 2
                    t = DMA("act", wgs_s[slw], wg_sl[slw], wsc[0, hc + 2],
                            after=b_wg[slw].wr() + tuple(fence))
                    b_wg[slw].setw(t)
                    t = DMA("act", wus_s[slw], wu_sl[slw], wsc[1, hc + 2],
                            after=b_wu[slw].wr() + tuple(fence))
                    b_wu[slw].setw(t)
                if hc == 12 and it + 1 < nIT:
                    emit_ln(it + 1, rps=(0, 1))
            emit_downA(dn_pend)
            for oc in range(4):
                g2 = E("act", lambda sc, o=oc: sc.activation(
                    gbf[:, o * 512:(o + 1) * 512], dn_ap[o][:, :], AF.Silu,
                    scale=1.0 / (127.0 * 127.0)),
                    after=dn_b[oc].rd() + b_gbf.wr(), signal=True)
                dn_b[oc].addr(g2)
                b_gbf.addw(g2)
                pt = E("dve", lambda v, o=oc: v.tensor_mul(
                    prod[:, o * 512:(o + 1) * 512], s_t[:, o * 512:(o + 1) * 512],
                    gbf[:, o * 512:(o + 1) * 512]),
                    after=(g2,) + b_st.rd() + b_prod.wr(), signal=True)
                b_gbf.addr(pt)
                b_st.addr(pt)
                b_prod.addw(pt)
            for hc in range(NHC):
                sl = hc % 2
                for oc in range(4):
                    dt = E("pe", lambda te, h=hc, o=oc, s=sl: te.matmul(
                        dn_ap[o][:, :], wdB_sl[s][:, o * 128:(o + 1) * 128],
                        hpr[:, h * 512:(h + 1) * 512],
                        start=(h == 0), stop=(h == NHC - 1), skip_group_check=True),
                        after=b_hpr.rd() + b_wdB[sl].rd()
                        + (dn_b[oc].wr() if hc == 0 else ()),
                        signal=(oc == 3))
                    if hc == 0 and oc == 3:
                        for o2 in range(4):
                            dn_b[o2].setw(dt)
                    elif hc == NHC - 1 and oc == 3:
                        for o2 in range(4):
                            dn_b[o2].addw(dt)
                    b_wdB[sl].addr(dt)
                    b_hpr.addr(dt)
                if hc + 2 < NHC:
                    loadB(hc + 2)
                if hc == 0 and it + 1 < nIT:
                    loadA(0)
                if hc == 1 and it + 1 < nIT:
                    loadA(1)
                if hc == NHC - 2 and it + 1 < nIT:
                    loadB(0)
                if hc == NHC - 1 and it + 1 < nIT:
                    loadB(1)
            for oc in range(4):
                oo = oc + 4
                g2 = E("act", lambda sc, o=oc: sc.activation(
                    gbf[:, o * 512:(o + 1) * 512], dn_ap[o][:, :], AF.Silu,
                    scale=1.0 / (127.0 * 127.0)),
                    after=dn_b[oc].rd() + b_gbf.wr(), signal=True)
                dn_b[oc].addr(g2)
                b_gbf.addw(g2)
                pt = E("dve", lambda v, o=oc, o2=oo: v.tensor_mul(
                    prod[:, o2 * 512:(o2 + 1) * 512], s_t[:, o2 * 512:(o2 + 1) * 512],
                    gbf[:, o * 512:(o + 1) * 512]),
                    after=(g2,) + b_st.rd() + b_prod.wr(), signal=True)
                b_gbf.addr(pt)
                b_st.addr(pt)
                b_prod.addw(pt)
            if it + 1 < nIT:
                emit_stile(it + 1)
            for rp in range(4):
                rt = it * 4 + rp
                slot = rt % 6
                tb = tp_ap[rp]
                bb = dn_b[rp]
                tt = None
                for oc in range(NDC):
                    tt = E("pe", lambda te, o=oc, p=rp, t_=tb: te.transpose(
                        t_[:, o * 128:(o + 1) * 128],
                        prod[:, o * 512 + p * 128: o * 512 + p * 128 + 128],
                        idn.ap()[:, :]),
                        after=b_prod.rd() + (bb.wr() if oc == 0 else ()),
                        signal=True)
                bb.setw(tt)
                b_prod.addr(tt)
                ot = E("dve", lambda v, s=slot, t_=tb: v.tensor_add(
                    xp6[s][:, :], xp6[s][:, :], t_[:, :]),
                    after=(tt,) + b_xp6[slot].rd() + b_xp6[slot].wr(), signal=True)
                bb.addr(ot)
                b_xp6[slot].setw(ot)
                st2 = DMA("sp", xp_sl[slot], out_d[rt * 128:(rt + 1) * 128, :],
                          xp6[slot][:, :], after=b_xp6[slot].rd())
                b_xp6[slot].addr(st2)
                if rp == 1 and it + 1 < nIT:
                    emit_ln(it + 1, rps=(2, 3))

        ndbg = 0
        if DEBUG_BUFS:
            reg = {
                "xnT": (xnT.ap()[:, :], [128, max(NDC * S, 32256)], BF16),
                "ctxT": (ctxT.ap()[:, :], [128, max(NDC * S, 8192)], BF16),
                "Qg": (Qg, [128, S], BF16),
                "Kg": (Kg, [128, S], BF16),
                "Vt": (Vt, [128, VT_COLS], BF16),
                "s_t": (s_t, [128, 4096], BF16),
                "hpr": (hpr, [128, 12288], BF16),
                "prod": (prod, [128, 4096], BF16),
                "gbf": (gbf, [128, 2048], BF16),
                "xnt_t": (xnt_t, [128, 4096], BF16),
                "lnsc": (lnsc.ap()[:, :], [128, nRT], F32),
                "lnbi": (lnbi.ap()[:, :], [128, nRT], F32),
            }
            endf = sch.fence() + [(out_s, 16 * nRT)]
            for nm in DEBUG_BUFS:
                if nm not in reg:
                    continue
                ap, shp, dt = reg[nm]
                dd = nc.dram_tensor("dbg_" + nm, shp, dt, kind="ExternalOutput").ap()
                DMA("sp", out_s, dd, ap, after=endf)
                ndbg += 1
        final_after = [(s_, sch.cnt[id(s_)]) for s_ in xp_sl]
        final_after.append((out_s, 16 * (ndbg + ndbg_e[0])))
        E("sp", lambda e: e.nop(), after=final_after)

        with nc.Block() as block:
            sch.replay(block)

    return nc


def _perm():
    idx = np.zeros(D, dtype=np.int64)
    for h in range(NH):
        for r in range(DK):
            idx[h * DK + r] = h * DK + (2 * r if r < 32 else 2 * (r - 32) + 1)
    return idx


def _host_prep(S, Wq, bq, Wk, bk, Wv, bv, Wo, bo, w_gate, w_up, w_down):
    pi = _perm()
    wq = np.ascontiguousarray(np.asarray(Wq, dtype=np.float32)[pi])
    wk = np.ascontiguousarray(np.asarray(Wk, dtype=np.float32)[pi])
    wv = np.asarray(Wv, dtype=np.float32)
    wo = np.asarray(Wo, dtype=np.float32)
    lay = lambda w, n: np.ascontiguousarray(
        w.reshape(n, 128, NDC, 128).transpose(0, 2, 3, 1))
    d = {
        "wq_l": lay(wq, NG), "wk_l": lay(wk, NG), "wv_l": lay(wv, NG),
        "wo_l": lay(wo, NDC),
        "bq_l": np.ascontiguousarray(np.asarray(bq, np.float32)[pi].reshape(NG, 128).T),
        "bk_l": np.ascontiguousarray(np.asarray(bk, np.float32)[pi].reshape(NG, 128).T),
        "bo_l": np.ascontiguousarray(np.asarray(bo, np.float32).reshape(NDC, 128).T),
        "bv_l": np.ascontiguousarray(np.asarray(bv, np.float32).reshape(1, D)),
        "wg_l": np.ascontiguousarray(
            np.asarray(w_gate, np.int32).reshape(NHC, 128, NDC, 128).transpose(0, 2, 3, 1)),
        "wu_l": np.ascontiguousarray(
            np.asarray(w_up, np.int32).reshape(NHC, 128, NDC, 128).transpose(0, 2, 3, 1)),
        "wd_l": np.ascontiguousarray(
            np.asarray(w_down, np.int32).T.reshape(NHC, 128, D)),
    }
    pos = np.arange(S, dtype=np.float32)
    kidx = np.arange(32, dtype=np.float32)
    ang = (pos[:, None] / (10000.0 ** (2.0 * kidx / 64.0))[None, :]).astype(np.float32)
    cp = np.cos(ang).T.astype(np.float32)
    sp = np.sin(ang).T.astype(np.float32)
    c64 = np.concatenate([cp, cp], 0)
    s64 = np.concatenate([-sp, sp], 0)
    d["ctab"] = np.ascontiguousarray(np.concatenate([c64, c64], 0)).astype(ml_dtypes.bfloat16)
    d["stab"] = np.ascontiguousarray(np.concatenate([s64, s64], 0)).astype(ml_dtypes.bfloat16)
    d["tri"] = np.ascontiguousarray(np.triu(np.ones((128, 128), np.float32))).astype(ml_dtypes.bfloat16)
    d["ident"] = np.eye(128, dtype=np.float32).astype(ml_dtypes.bfloat16)
    return d


def kernel(L0_attn_x, L0_attn_Wq, L0_attn_bq, L0_attn_Wk, L0_attn_bk, L0_attn_Wv,
           L0_attn_bv, L0_attn_Wo, L0_attn_bo, L0_ffn_w_gate, L0_ffn_w_up,
           L0_ffn_w_down):
    x = np.asarray(L0_attn_x, dtype=np.float32)
    Bn, S, _ = x.shape
    shared = _host_prep(S, L0_attn_Wq, L0_attn_bq, L0_attn_Wk, L0_attn_bk,
                        L0_attn_Wv, L0_attn_bv, L0_attn_Wo, L0_attn_bo,
                        L0_ffn_w_gate, L0_ffn_w_up, L0_ffn_w_down)
    nc = build_program(S)
    in_maps = [dict(shared, x=np.ascontiguousarray(x[b])) for b in range(Bn)]
    res = run_bass_kernel_spmd(nc, in_maps, core_ids=list(range(Bn)))
    global LAST_DEBUG
    LAST_DEBUG = {nm: np.stack([np.asarray(res.results[b]["dbg_" + nm], dtype=np.float32)
                                for b in range(Bn)], 0)
                  for nm in DEBUG_BUFS}
    return np.stack([res.results[b]["out"] for b in range(Bn)], 0)


# version 20
# speedup vs baseline: 1.2016x; 1.2016x over previous
import contextlib
import numpy as np
import ml_dtypes
import concourse.bass as bass
import concourse.mybir as mybir
from concourse.bass_utils import run_bass_kernel_spmd

F32 = mybir.dt.float32
BF16 = mybir.dt.bfloat16
I32 = mybir.dt.int32
AF = mybir.ActivationFunctionType
ALU = mybir.AluOpType

D = 1024
NH = 16
DK = 64
NG = 8
HID = 3072
NHC = HID // 128
NDC = D // 128

ENGMAP = {"pe": "tensor", "act": "scalar", "dve": "vector", "pool": "gpsimd", "sp": "sync"}

DEBUG_BUFS = []
LAST_DEBUG = {}


class Buf:
    def __init__(self):
        self.w = []
        self.r = []

    def rd(self):
        return tuple(self.w)

    def wr(self):
        return tuple(self.w) + tuple(self.r)

    def setw(self, tok):
        self.w = [tok] if tok else []
        self.r = []

    def addw(self, tok):
        if tok:
            self.w.append(tok)

    def addr(self, tok):
        if tok:
            self.r.append(tok)


class Sched:
    def __init__(self, nc, stack):
        self.nc = nc
        self.stack = stack
        self.q = {e: [] for e in ENGMAP}
        self.cnt = {}
        self.know = {e: {} for e in ENGMAP}
        self.esem = {}
        for e in ("pe", "act", "dve", "pool"):
            self.esem[e] = self.sem("sem_" + e)

    def sem(self, name):
        h = self.stack.enter_context(self.nc.semaphore(name))
        self.cnt[id(h)] = 0
        return h

    def emit(self, eng, fn, after=(), signal=None, inc=1):
        own = self.esem.get(eng)
        skip = own if eng == "pe" else None
        waits = {}
        for tok in after:
            if tok is None:
                continue
            sem, val = tok
            if sem is skip:
                continue
            k = id(sem)
            if self.know[eng].get(k, 0) >= val:
                continue
            if k not in waits or waits[k][1] < val:
                waits[k] = (sem, val)
        for k, (sem, val) in waits.items():
            self.know[eng][k] = val
        tok = None
        incs = []
        if signal is not None and signal is not False:
            sems = signal if isinstance(signal, list) else [signal]
            for s_ in sems:
                s_ = own if s_ is True else s_
                self.cnt[id(s_)] += inc
                if tok is None:
                    tok = (s_, self.cnt[id(s_)])
                incs.append((s_, inc))
        self.q[eng].append((fn, list(waits.values()), incs))
        return tok

    def dma(self, eng, sem, out, in_, after=()):
        return self.emit(eng, lambda e, o=out, i=in_: e.dma_start(out=o, in_=i),
                         after=after, signal=sem, inc=16)

    def fence(self):
        return [(s, self.cnt[id(s)]) for s in self.esem.values()]

    def replay(self, block):
        for eng, bassname in ENGMAP.items():
            ops = self.q[eng]

            def body(engine, ops=ops):
                for fn, waits, incs in ops:
                    for sem, val in waits:
                        engine.wait_ge(sem, val)
                    ins = fn(engine)
                    for sem, n in incs:
                        ins.then_inc(sem, n)

            getattr(block, bassname)(body)


def build_program(S):
    nIT = S // 512
    nRT = S // 128
    nJT = S // 128
    CH = min(1024, S)
    NCH = S // CH
    nc = bass.Bass("TRN2", target_bir_lowering=False, debug=False)

    xin = nc.dram_tensor("x", [S, D], F32, kind="ExternalInput").ap()
    wq_l = nc.dram_tensor("wq_l", [NG, NDC, 128, 128], F32, kind="ExternalInput").ap()
    wk_l = nc.dram_tensor("wk_l", [NG, NDC, 128, 128], F32, kind="ExternalInput").ap()
    wv_l = nc.dram_tensor("wv_l", [NG, NDC, 128, 128], F32, kind="ExternalInput").ap()
    wo_l = nc.dram_tensor("wo_l", [NDC, NDC, 128, 128], F32, kind="ExternalInput").ap()
    bq_l = nc.dram_tensor("bq_l", [128, NG], F32, kind="ExternalInput").ap()
    bk_l = nc.dram_tensor("bk_l", [128, NG], F32, kind="ExternalInput").ap()
    bo_l = nc.dram_tensor("bo_l", [128, NDC], F32, kind="ExternalInput").ap()
    bv_l = nc.dram_tensor("bv_l", [1, D], F32, kind="ExternalInput").ap()
    ctab_d = nc.dram_tensor("ctab", [128, S], BF16, kind="ExternalInput").ap()
    stab_d = nc.dram_tensor("stab", [128, S], BF16, kind="ExternalInput").ap()
    tri_d = nc.dram_tensor("tri", [128, 128], BF16, kind="ExternalInput").ap()
    id_d = nc.dram_tensor("ident", [128, 128], BF16, kind="ExternalInput").ap()
    wg_l = nc.dram_tensor("wg_l", [NHC, NDC, 128, 128], I32, kind="ExternalInput").ap()
    wu_l = nc.dram_tensor("wu_l", [NHC, NDC, 128, 128], I32, kind="ExternalInput").ap()
    wd_l = nc.dram_tensor("wd_l", [NHC, 128, D], I32, kind="ExternalInput").ap()
    wsc = nc.dram_tensor("wsc", [3, NHC, 128, D], BF16, kind="Internal").ap()
    out_d = nc.dram_tensor("out", [S, D], F32, kind="ExternalOutput").ap()

    stack = contextlib.ExitStack()
    with stack:
        sb = lambda name, shape, dt: stack.enter_context(nc.sbuf_tensor(name, shape, dt))
        pst = lambda name, shape, dt=F32: stack.enter_context(nc.psum_tensor(name, shape, dt))

        xnT = sb("xnT", [128, max(NDC * S, 32256)], BF16)
        ctxT = sb("ctxT", [128, max(NDC * S, 8192)], BF16)
        SCR = sb("SCR", [128, 28928], BF16)
        ctab = sb("ctab_s", [128, S], BF16)
        stab = sb("stab_s", [128, S], BF16)
        tri = sb("tri_s", [128, 128], BF16)
        idn = sb("id_s", [128, 128], BF16)
        bqs = sb("bqs", [128, NG], F32)
        bks = sb("bks", [128, NG], F32)
        bos = sb("bos", [128, NDC], F32)
        onesr = sb("onesr", [1, 128], BF16)
        lnsc = sb("lnsc", [128, nRT], F32)
        lnbi = sb("lnbi", [128, nRT], F32)
        lnst = sb("lnst", [128, 24], F32)
        lnag = sb("lnag", [128, 4], F32)
        lnsq = sb("lnsq", [128, 2], F32)
        epsv = sb("epsv", [128, 1], F32)
        ceb = sb("ceb", [128, 1024], BF16)
        rct = sb("rct", [128, 2048], BF16)
        rcb2 = [rct.ap()[0:1, 0:512], rct.ap()[0:1, 512:1024]]
        rcbf2 = [rct.ap()[64:65, 0:1024].bitcast(F32), rct.ap()[64:65, 1024:2048].bitcast(F32)]
        bvs = rct.ap()[0:1, 1024:2048]

        Qg = SCR.ap()[:, 0:S]
        Kg = SCR.ap()[:, S:2 * S]
        VT_COLS = nJT * 130
        VT_PAD = 64
        VT_HALF = VT_COLS + VT_PAD
        Vt2 = [SCR.ap()[:, 2 * S + h * VT_HALF: 2 * S + (h + 1) * VT_HALF]
               for h in range(2)]
        base = 2 * S + 2 * VT_HALF
        rsc_a = SCR.ap()[:, base:base + CH]
        rsc_b = SCR.ap()[:, base + 1024:base + 1024 + CH]
        Pslot = [SCR.ap()[:, base + 2048 + i * 3072: base + 2048 + (i + 1) * 3072]
                 for i in range(2)]
        wq_sl = SCR.ap()[:, base + 8192:base + 9216]
        wk_sl = SCR.ap()[:, base + 9216:base + 10240]
        wv_sl = SCR.ap()[:, base + 10240:base + 12288]
        assert base + 12288 <= 28928, "SCR overflow (P1)"

        xsl = [ctxT.ap()[:, i * 2048:(i + 1) * 2048].bitcast(F32) for i in range(3)]
        xnb0 = [ctxT.ap()[:, 6144 + i * 1024: 6144 + (i + 1) * 1024] for i in range(2)]

        wo_bf = SCR.ap()[:, 0:8192]
        wg_sl = [SCR.ap()[:, 8192 + i * 1024:8192 + (i + 1) * 1024] for i in range(2)]
        wu_sl = [SCR.ap()[:, 10240 + i * 1024:10240 + (i + 1) * 1024] for i in range(2)]
        wdA_sl = [SCR.ap()[:, 12288 + i * 512:12288 + (i + 1) * 512] for i in range(2)]
        wdB_sl = [SCR.ap()[:, 13312 + i * 512:13312 + (i + 1) * 512] for i in range(2)]
        xp3 = [SCR.ap()[:, 14336 + i * 2048:14336 + (i + 1) * 2048].bitcast(F32)
               for i in range(4)]
        obuf = [SCR.ap()[:, 22528 + i * 2048:22528 + (i + 1) * 2048].bitcast(F32)
                for i in range(2)]
        hpr = xnT.ap()[:, 0:12288]
        s_t = xnT.ap()[:, 12288:16384]
        prod = xnT.ap()[:, 16384:20480]
        xnt_t = xnT.ap()[:, 20480:24576]
        sgb = xnT.ap()[:, 24576:25088]
        gbf = xnT.ap()[:, 25088:27136]
        xnb3 = xnT.ap()[:, 27136:28160]

        S6 = pst("S6", [128, 3072], F32)
        C0 = pst("C0", [128, 512], F32)
        C1 = pst("C1", [128, 512], F32)
        S6_bf = S6.ap().bitcast(BF16)
        C0_bf = C0.ap().bitcast(BF16)

        sch = Sched(nc, stack)
        xq_sl = [sch.sem(f"xq{i}") for i in range(3)]
        xp_sl = [sch.sem(f"xp{i}") for i in range(6)]
        wqs_s = sch.sem("wqs")
        wks_s = sch.sem("wks")
        wvs_s = sch.sem("wvs")
        wgs_s = [sch.sem(f"wgs{i}") for i in range(2)]
        wus_s = [sch.sem(f"wus{i}") for i in range(2)]
        wdAs_s = [sch.sem(f"wdA{i}") for i in range(2)]
        wdBs_s = [sch.sem(f"wdB{i}") for i in range(2)]
        sw_s = sch.sem("sw")
        aw_s = sch.sem("aw")
        cs_s = sch.sem("cs")
        cv_s = sch.sem("cv")
        sc_s = sch.sem("scw")
        fw_s = sch.sem("fw")
        fd_s = sch.sem("fd")
        out_s = sch.sem("outs")

        B = Buf
        b_xnT, b_ctxT = B(), B()
        b_Qg, b_Kg = B(), B()
        b_Vt2 = [B(), B()]
        b_rsa, b_rsb, b_t1 = B(), B(), B()
        b_P = [B(), B()]
        b_wq, b_wk, b_wv = B(), B(), B()
        b_xsl = [B(), B(), B()]
        b_xnb0 = [B(), B()]

        b_ln = B()
        b_lnag = [B(), B()]
        b_lnsq = [B(), B()]
        b_ceb = [B(), B()]
        b_rcs = [B(), B()]
        b_S6 = [B() for _ in range(6)]
        b_SAb = b_S6[0:3]
        b_SBb = b_S6[3:6]
        b_C = [B(), B()]
        b_wo = B()
        b_wg = [B(), B()]
        b_wu = [B(), B()]
        b_wdA = [B(), B()]
        b_wdB = [B(), B()]
        b_xp3 = [B(), B(), B(), B()]
        b_ob = [B(), B()]
        b_hpr, b_st, b_prod, b_xnt3, b_sgb, b_gbf, b_xnb3 = B(), B(), B(), B(), B(), B(), B()

        E = sch.emit
        DMA = sch.dma

        DMA("pool", cs_s, ctab.ap()[:, :], ctab_d)
        DMA("pool", cs_s, stab.ap()[:, :], stab_d)
        DMA("pool", cs_s, tri.ap()[:, :], tri_d)
        DMA("pool", cs_s, idn.ap()[:, :], id_d)
        DMA("pool", cs_s, bqs.ap()[:, :], bq_l)
        DMA("pool", cs_s, bks.ap()[:, :], bk_l)
        DMA("pool", cs_s, bos.ap()[:, :], bo_l)
        DMA("pool", cs_s, bvs, bv_l)
        cs_tok = (cs_s, 16 * 8)

        E("pool", lambda g: g.memset(onesr.ap()[:, :], 1.0), signal=True)
        eps_tok = E("pool", lambda g: g.memset(epsv.ap()[:, :], 1e-5), signal=True)
        vt3h = [Vt2[h][:, 0:VT_COLS].rearrange("p (t c) -> p t c", t=nJT, c=130)
                for h in range(2)]
        E("pool", lambda g: g.memset(Vt2[0][:, VT_COLS:VT_HALF], 0.0), signal=True)
        E("pool", lambda g: g.memset(Vt2[1][:, VT_COLS:VT_HALF], 0.0), signal=True)
        E("pool", lambda g: g.memset(vt3h[0][:, :, 64:65], 1.0), signal=True)
        E("pool", lambda g: g.memset(vt3h[0][:, :, 129:130], 1.0), signal=True)
        E("pool", lambda g: g.memset(vt3h[1][:, :, 64:65], 1.0), signal=True)
        mem_tok = E("pool", lambda g: g.memset(vt3h[1][:, :, 129:130], 1.0), signal=True)
        b_Vt2[0].addw(mem_tok)
        b_Vt2[1].addw(mem_tok)


        rr = lambda ap: ap.rearrange("a p c -> p a c")
        t = DMA("pool", wqs_s, wq_sl.rearrange("p (a c) -> p a c", a=NDC, c=128), rr(wq_l[0]))
        b_wq.setw(t)
        t = DMA("pool", wks_s, wk_sl.rearrange("p (a c) -> p a c", a=NDC, c=128), rr(wk_l[0]))
        b_wk.setw(t)
        wv4 = wv_sl.rearrange("p (a b c) -> p a b c", a=NDC, b=2, c=128)
        t = DMA("pool", wvs_s, wv4[:, :, 0, :], rr(wv_l[0]))
        b_wv.setw(t)
        t = DMA("pool", wvs_s, wv4[:, :, 1, :], rr(wv_l[1]))
        b_wv.addw(t)

        p0ps = [S6_bf[:, 0:1024], S6_bf[:, 1024:2048]]
        b_p0ps = [b_SAb[0], b_SAb[1]]
        xnT4 = xnT.ap()[:, 0:NDC * S].rearrange("p (a r c) -> p a r c", a=NDC, r=nRT, c=128)

        for rt in range(nRT):
            sl = rt % 3
            sl2 = rt % 2
            q0 = rt % 2
            lnst_s = lnst.ap()[:, q0 * 12:(q0 + 1) * 12]
            lnag_s = lnag.ap()[:, q0 * 2:(q0 + 1) * 2]
            lnsq_s = lnsq.ap()[:, q0:q0 + 1]
            b_lnag_s = b_lnag[q0]
            b_lnsq_s = b_lnsq[q0]
            xt = DMA("sp", xq_sl[sl], xsl[sl][:, :], xin[rt * 128:(rt + 1) * 128, :],
                     after=b_xsl[sl].wr())
            b_xsl[sl].setw(xt)
            ts1 = E("dve", lambda v, s=sl, L=lnst_s: v.bn_stats(L[:, 0:6], xsl[s][:, 0:512]),
                    after=b_xsl[sl].rd(), signal=True)
            ts2 = E("dve", lambda v, s=sl, L=lnst_s: v.bn_stats(L[:, 6:12], xsl[s][:, 512:1024]),
                    after=b_xsl[sl].rd(), signal=True)
            agt = E("dve", lambda v, L=lnst_s, A=lnag_s: v.bn_aggr(A[:, :], L[:, :]),
                    after=(ts1, ts2) + b_lnag_s.wr(), signal=True)
            b_lnag_s.setw(agt)
            sqt = E("act", lambda sc, A=lnag_s, Q=lnsq_s: sc.activation(
                Q[:, :], A[:, 1:2], AF.Sqrt, bias=epsv.ap()[:, :]),
                    after=(agt, cs_tok, eps_tok) + b_lnsq_s.wr(), signal=True)
            b_lnag_s.addr(sqt)
            b_lnsq_s.setw(sqt)
            rct2 = E("dve", lambda v, r=rt, Q=lnsq_s: v.reciprocal(lnsc.ap()[:, r:r + 1], Q[:, :]),
                     after=(sqt,), signal=True)
            b_lnsq_s.addr(rct2)
            stt = E("dve", lambda v, r=rt, A=lnag_s: v.scalar_tensor_tensor(
                lnbi.ap()[:, r:r + 1], A[:, 0:1], -1.0,
                lnsc.ap()[:, r:r + 1], ALU.mult, ALU.mult),
                    after=(rct2,), signal=True)
            b_lnag_s.addr(stt)
            b_ln.addw(stt)
            xnbt = E("act", lambda sc, s=sl, s2=sl2, r=rt: sc.activation(
                xnb0[s2][:, :], xsl[s][:, :], AF.Identity,
                bias=lnbi.ap()[:, r:r + 1], scale=lnsc.ap()[:, r:r + 1]),
                after=(stt,) + b_xsl[sl].rd() + b_xnb0[sl2].wr(), signal=True)
            b_xsl[sl].addr(xnbt)
            b_xnb0[sl2].setw(xnbt)
            ptok = None
            for dc in range(NDC):
                ptok = E("pe", lambda te, s2=sl2, d=dc: te.transpose(
                    p0ps[s2][:, d * 128:(d + 1) * 128],
                    xnb0[s2][:, d * 128:(d + 1) * 128], idn.ap()[:, :]),
                    after=(cs_tok, xnbt) + (b_p0ps[sl2].wr() if dc == 0 else ()),
                    signal=True)
            b_xnb0[sl2].addr(ptok)
            b_p0ps[sl2].setw(ptok)
            psv = p0ps[sl2].rearrange("p (a c) -> p a c", a=NDC, c=128)
            ctok = E("dve", lambda v, r=rt, pv=psv: v.tensor_copy(
                xnT4[:, :, r, :], pv[:, :, :]),
                after=(ptok,), signal=True)
            b_p0ps[sl2].addr(ctok)
            b_xnT.addw(ctok)

        ndbg_e = [0]
        dbg_extra = []
        def early_dump(nm, ap, shp, dt):
            if nm in DEBUG_BUFS:
                dd = nc.dram_tensor("dbg_" + nm, shp, dt, kind="ExternalOutput").ap()
                tok = DMA("sp", out_s, dd, ap, after=sch.fence())
                ndbg_e[0] += 1
                dbg_extra.append(tok)
                return tok
            return None
        t0 = early_dump("xnT0", xnT.ap()[:, 0:NDC * S], [128, NDC * S], BF16)
        if t0 is not None:
            b_xnT.addr(t0)
        for t in sch.fence():
            b_ctxT.addr(t)

        def qkv_qk(g, which):
            w_ap = wq_sl if which == "q" else wk_sl
            b_w = b_wq if which == "q" else b_wk
            dst = Qg if which == "q" else Kg
            b_dst = b_Qg if which == "q" else b_Kg
            bias = bqs if which == "q" else bks
            for half in range(NCH):
                etoks = []
                for k4 in range(CH // 512):
                    it = half * (CH // 512) + k4
                    reg = S6.ap()[:, (k4 % 2) * 512:(k4 % 2) * 512 + 512]
                    bank = b_SAb[k4 % 2]
                    mmt = None
                    for dc in range(NDC):
                        mmt = E("pe", lambda te, i=it, d=dc, r=reg, w=w_ap: te.matmul(
                            r[:, :], w[:, d * 128:(d + 1) * 128],
                            xnT.ap()[:, d * S + i * 512: d * S + (i + 1) * 512],
                            start=(d == 0), stop=(d == NDC - 1), skip_group_check=True),
                            after=b_xnT.rd() + b_w.rd() + (bank.wr() if dc == 0 else ()),
                            signal=(dc == NDC - 1))
                    bank.setw(mmt)
                    b_w.addr(mmt)
                    et = E("dve", lambda v, k_=k4, r=reg, g_=g, b_=bias: v.tensor_scalar(
                        rsc_a[:, k_ * 512:(k_ + 1) * 512], r[:, :],
                        b_.ap()[:, g_:g_ + 1], None, ALU.add),
                        after=(mmt, cs_tok) + b_rsa.wr(), signal=True)
                    bank.addr(et)
                    etoks.append(et)
                b_rsa.setw(etoks[0])
                for e2 in etoks[1:]:
                    b_rsa.addw(e2)
                stoks = []
                for (d0, s0) in ((0, 32), (32, 0), (64, 96), (96, 64)):
                    st = DMA("sp", sw_s, rsc_b[d0:d0 + 32, :], rsc_a[s0:s0 + 32, :],
                             after=b_rsa.rd() + b_rsb.wr())
                    stoks.append(st)
                b_rsb.setw(stoks[0])
                for s2 in stoks[1:]:
                    b_rsb.addw(s2)
                for s2 in stoks:
                    b_rsa.addr(s2)
                co = half * CH
                t1 = E("dve", lambda v, c=co, d_=dst: v.tensor_mul(
                    d_[:, c:c + CH], ctab.ap()[:, c:c + CH], rsc_a[:, :]),
                    after=(cs_tok,) + b_rsa.rd() + b_dst.wr(), signal=True)
                b_rsa.addr(t1)
                t2 = E("dve", lambda v, c=co: v.tensor_mul(
                    rsc_a[:, :], stab.ap()[:, c:c + CH], rsc_b[:, :]),
                    after=(cs_tok, t1) + b_rsb.rd() + b_rsa.wr(), signal=True)
                b_rsa.setw(t2)
                b_rsb.addr(t2)
                t3 = E("dve", lambda v, c=co, d_=dst: v.tensor_add(
                    d_[:, c:c + CH], d_[:, c:c + CH], rsc_a[:, :]),
                    after=(t1, t2) + b_dst.wr(), signal=True)
                b_rsa.addr(t3)
                b_dst.addw(t3)

        def qkv_v(g):
            for jt in range(nJT):
                reg = (C0 if jt % 2 == 0 else C1).ap()[:, 0:256]
                bank = b_C[jt % 2]
                for dc in range(NDC):
                    E("pe", lambda te, j=jt, d=dc, r=reg: te.matmul(
                        r[:, :], xnT.ap()[:, d * S + j * 128: d * S + j * 128 + 128],
                        wv_sl[:, d * 256:(d + 1) * 256],
                        start=(d == 0), stop=False, skip_group_check=True),
                        after=b_xnT.rd() + b_wv.rd() + (bank.wr() if dc == 0 else ()),
                        signal=False)
                mmt = E("pe", lambda te, r=reg, g_=g: te.matmul(
                    r[:, :], onesr.ap()[:, :], bvs[:, g_ * 128:(g_ + 2) * 128],
                    start=False, stop=True, skip_group_check=True),
                    after=(mem_tok, cs_tok), signal=True)
                bank.setw(mmt)
                b_wv.addr(mmt)
                dst0 = Vt2[0][:, jt * 130: jt * 130 + 130].rearrange(
                    "p (b c) -> p b c", b=2, c=65)
                dst1 = Vt2[1][:, jt * 130: jt * 130 + 130].rearrange(
                    "p (b c) -> p b c", b=2, c=65)
                srcv = reg.rearrange("p (h b c) -> p h b c", h=2, b=2, c=64)
                et = E("act", lambda sc, d_=dst0, s_=srcv: sc.activation(
                    d_[:, :, 0:64], s_[:, 0, :, :], AF.Copy),
                    after=(mmt,) + b_Vt2[0].wr(), signal=True)
                bank.addr(et)
                b_Vt2[0].addw(et)
                et = E("act", lambda sc, d_=dst1, s_=srcv: sc.activation(
                    d_[:, :, 0:64], s_[:, 1, :, :], AF.Copy),
                    after=(mmt,) + b_Vt2[1].wr(), signal=True)
                bank.addr(et)
                b_Vt2[1].addw(et)

        sgrp = [0]
        cpar = [0]
        fin_i = [0]

        fin_pending = []

        def flush_fin():
            while fin_pending:
                fin_pending.pop(0)()

        def emit_finalize(Cc, b_Cc, last_ctx_tok, g, h2, it):
            cslot = fin_i[0] % 2
            fin_i[0] += 1
            ev = E("dve", lambda v, C_=Cc, s=cslot: v.tensor_copy(
                ceb.ap()[:, s * 512:(s + 1) * 512][0:64, :], C_.ap()[0:64, :]),
                after=(last_ctx_tok,) + b_ceb[cslot].wr(), signal=True)
            b_Cc.addr(ev)
            b_ceb[cslot].setw(ev)
            rc = E("act", lambda sc, C_=Cc, s=cslot: sc.activation(
                rcbf2[s][:, :], C_.ap()[64:65, :], AF.Ln),
                   after=(ev,) + b_rcs[cslot].wr(), signal=True)
            b_Cc.addr(rc)
            rc = E("act", lambda sc, s=cslot: sc.activation(
                rcb2[s][:, :], rcbf2[s][:, :], AF.Exp, scale=-1.0),
                   after=(rc,), signal=True)
            b_rcs[cslot].setw(rc)

            def doit():
                bc = E("pe", lambda te, C_=Cc, s=cslot: te.matmul(
                    C_.ap()[0:64, 0:512], onesr.ap()[0:1, 0:64], rcb2[s][:, :],
                    start=True, stop=True, skip_group_check=True),
                    after=(rc,) + b_Cc.wr(), signal=True)
                b_Cc.setw(bc)
                b_rcs[cslot].addr(bc)
                cn = E("dve", lambda v, C_=Cc, s=cslot, g_=g, h_=h2, i_=it: v.tensor_mul(
                    ctxT.ap()[64 * h_:64 * h_ + 64,
                              g_ * S + i_ * 512: g_ * S + (i_ + 1) * 512],
                    ceb.ap()[:, s * 512:(s + 1) * 512][0:64, :], C_.ap()[0:64, 0:512]),
                    after=(ev, bc) + b_ctxT.wr(), signal=True)
                b_Cc.addr(cn)
                b_ceb[cslot].addr(cn)
                b_ctxT.addw(cn)

            fin_pending.append(doit)

        def attention(g):
            b_Vt = b_Vt2[g % 2]
            Vt = Vt2[g % 2]
            """Both heads of group g, h1 staggered one score-group behind h0.

            PSUM: S6 banks 0-2 = h0 scores, banks 3-5 = h1 scores; C0/C1 = ctx.
            P slots: per step parity, h0 at [0:1536], h1 at [1536:3072]."""
            for it in range(nIT):
                nfull = 4 * it
                groups = []
                j0 = 0
                while j0 < nfull:
                    n = min(3, nfull - j0)
                    groups.append([(j0 + k, 512, 0, k * 512) for k in range(n)])
                    j0 += n
                groups.append([(4 * it + 0, 512, 0, 0),
                               (4 * it + 1, 384, 128, 512),
                               (4 * it + 3, 128, 384, 896),
                               (4 * it + 2, 256, 256, 1024)])
                total = len(groups)
                ndiag = total - 1
                first_ctx = [True, True]
                last_ctx_tok = [None, None]
                pend = {}

                def step(k):
                    specs = []
                    for h2, gk in ((0, k), (1, k - 1)):
                        if not (0 <= gk < total):
                            continue
                        grp = groups[gk]
                        is_diag = (gk == ndiag)
                        Sreg = S6.ap()[:, h2 * 1536:(h2 + 1) * 1536]
                        b_Sb = b_S6[h2 * 3:(h2 + 1) * 3]
                        klo = 64 * h2
                        wtot = sum(w for (_, w, _, _) in grp)
                        dep0 = (b_Qg.rd() + b_Kg.rd() + b_Sb[0].wr() + b_Sb[1].wr()
                                + b_Sb[2].wr())
                        specs.append([h2, gk, grp, Sreg, b_Sb, klo, dep0, wtot,
                                      is_diag, None])
                    par = k % 2
                    Pb = Pslot[par]
                    b_Pb = b_P[par]
                    maxlen = max((len(s[2]) for s in specs), default=0)
                    for i in range(maxlen):
                        for sp_ in specs:
                            h2, gk, grp, Sreg, b_Sb, klo, dep0, wtot, is_diag, _ = sp_
                            if i >= len(grp):
                                continue
                            jt, w, r, off = grp[i]
                            tok = E("pe", lambda te, j=jt, w_=w, r_=r, o=off,
                                    S_=Sreg, i_=it, kl=klo: te.matmul(
                                S_[0:128, o:o + w_],
                                Kg[kl:kl + 64, j * 128:(j + 1) * 128],
                                Qg[kl:kl + 64, i_ * 512 + r_: i_ * 512 + r_ + w_],
                                start=True, stop=True, skip_group_check=True,
                                tile_position=(kl, 0)),
                                after=(dep0 if i == 0 else ()),
                                signal=(i == len(grp) - 1))
                            if i == len(grp) - 1:
                                sp_[9] = tok
                    for sp_ in specs:
                        h2, gk, grp, Sreg, b_Sb, klo, dep0, wtot, is_diag, stok = sp_
                        for bb in b_Sb:
                            bb.setw(stok)
                        b_Qg.addr(stok)
                        b_Kg.addr(stok)
                    merged = (len(specs) == 2 and specs[0][7] == 1536
                              and specs[1][7] == 1536 and not specs[0][8]
                              and not specs[1][8])
                    if merged:
                        etok = E("act", lambda sc, P_=Pb: sc.activation(
                            P_[:, 0:3072], S6.ap()[:, 0:3072], AF.Exp, scale=0.125),
                            after=(specs[0][9], specs[1][9]) + b_Pb.wr(), signal=True)
                        for bb in b_S6:
                            bb.addr(etok)
                        b_Pb.setw(etok)
                        etoks = {0: etok, 1: etok}
                        mtoks = {0: etok, 1: etok}
                    else:
                        etoks = {}
                        mtoks = {}
                        for sp_ in specs:
                            h2, gk, grp, Sreg, b_Sb, klo, dep0, wtot, is_diag, stok = sp_
                            wexp = 1280 if is_diag else wtot
                            etok = E("act", lambda sc, w_=wexp, h_=h2, P_=Pb: sc.activation(
                                P_[:, h_ * 1536: h_ * 1536 + w_],
                                S6.ap()[:, h_ * 1536: h_ * 1536 + w_],
                                AF.Exp, scale=0.125),
                                after=(stok,) + b_Pb.wr(), signal=True)
                            for bb in b_Sb:
                                bb.addr(etok)
                            b_Pb.addw(etok)
                            etoks[h2] = etok
                            mtoks[h2] = etok
                    for sp_ in specs:
                        h2, gk, grp, Sreg, b_Sb, klo, dep0, wtot, is_diag, stok = sp_
                        if is_diag:
                            mtok = etoks[h2]
                            for kk, (jt, w, r, off) in enumerate(grp):
                                mtok = E("dve", lambda v, o=off, h_=h2, P_=Pb: v.tensor_mul(
                                    P_[:, h_ * 1536 + o: h_ * 1536 + o + 128],
                                    P_[:, h_ * 1536 + o: h_ * 1536 + o + 128],
                                    tri.ap()[:, :]),
                                    after=(etoks[h2], cs_tok), signal=(kk == len(grp) - 1))
                            b_Pb.addw(mtok)
                            mtoks[h2] = mtok
                        pend[(h2, gk)] = (grp, par, mtoks[h2], is_diag)
                    if k == 1:
                        flush_fin()
                    for h2, gk in ((0, k - 1), (1, k - 2)):
                        if (h2, gk) not in pend:
                            continue
                        grp, ppar, mtok, is_diag = pend.pop((h2, gk))
                        Pb2 = Pslot[ppar]
                        b_Pb2 = b_P[ppar]
                        Cc = C0 if h2 == 0 else C1
                        b_Cc = b_C[h2]
                        vo = 65 * h2
                        for kk, (jt, w, r, off) in enumerate(grp):
                            dep = (mtok,) + b_Vt.rd()
                            if first_ctx[h2]:
                                dep = dep + b_Cc.wr()
                            last_ctx_tok[h2] = E("pe", lambda te, j=jt, w_=w, r_=r,
                                                 o=off, P_=Pb2, C_=Cc, v_=vo, h_=h2,
                                                 st=first_ctx[h2],
                                                 sp=(is_diag and kk == len(grp) - 1): te.matmul(
                                C_.ap()[0:128, r_:r_ + w_],
                                Vt[:, j * 130 + v_: j * 130 + v_ + 128],
                                P_[:, h_ * 1536 + o: h_ * 1536 + o + w_],
                                start=st, stop=sp, skip_group_check=True),
                                after=dep,
                                signal=(is_diag and kk == len(grp) - 1))
                            first_ctx[h2] = False
                        b_Pb2.addr(last_ctx_tok[h2])
                        b_Vt.addr(last_ctx_tok[h2])
                        if is_diag:
                            b_C[h2].setw(last_ctx_tok[h2])
                            emit_finalize(Cc, b_C[h2], last_ctx_tok[h2], g, h2, it)

                for k in range(total + 2):
                    step(k)

        conv_list = [(t, hc) for t in range(3) for hc in range(NHC)]
        conv_i = [0]

        def emit_conversions(n):
            for _ in range(n):
                if conv_i[0] >= len(conv_list):
                    return
                tt, hc = conv_list[conv_i[0]]
                conv_i[0] += 1
                sl = conv_i[0] % 2
                if tt == 0:
                    srcp = wg_l[hc].rearrange("a p c -> p a c")
                elif tt == 1:
                    srcp = wu_l[hc].rearrange("a p c -> p a c")
                else:
                    srcp = wd_l[hc]
                dstp = (wsc[tt, hc].rearrange("p (a c) -> p a c", a=NDC, c=128)
                        if tt < 2 else wsc[tt, hc])
                DMA("pool", sc_s, dstp, srcp)

        for g in range(NG):
            qkv_qk(g, "q")
            qkv_qk(g, "k")
            if g % 2 == 0:
                qkv_v(g)
            if g == 0:
                tq0 = early_dump("Qg0", Qg, [128, S], BF16)
                if tq0 is not None:
                    b_Qg.addr(tq0)
                tk0 = early_dump("Kg0", Kg, [128, S], BF16)
                if tk0 is not None:
                    b_Kg.addr(tk0)
                tv0 = early_dump("Vt0", Vt2[0][:, 0:VT_COLS], [128, VT_COLS], BF16)
                if tv0 is not None:
                    b_Vt2[0].addr(tv0)
            ptoks = None
            if g + 1 < NG:
                tq = DMA("pool", wqs_s, wq_sl.rearrange("p (a c) -> p a c", a=NDC, c=128),
                         rr(wq_l[g + 1]), after=b_wq.wr())
                tk = DMA("pool", wks_s, wk_sl.rearrange("p (a c) -> p a c", a=NDC, c=128),
                         rr(wk_l[g + 1]), after=b_wk.wr())
                ptoks = [tq, tk]
                if g % 2 == 1:
                    tv = DMA("pool", wvs_s, wv4[:, :, 0, :], rr(wv_l[g + 1]),
                             after=b_wv.wr())
                    tv2 = DMA("pool", wvs_s, wv4[:, :, 1, :], rr(wv_l[g + 2]),
                              after=b_wv.wr())
                    ptoks += [tv, tv2]
            emit_conversions(9 if g < NG - 1 else 100)
            attention(g)
            if ptoks:
                b_wq.setw(ptoks[0])
                b_wk.setw(ptoks[1])
                if len(ptoks) > 2:
                    b_wv.setw(ptoks[2])
                    b_wv.addw(ptoks[3])

        flush_fin()
        fence = sch.fence()
        sc_tok = (sc_s, 16 * len(conv_list))
        fence.append(sc_tok)

        wot = DMA("pool", aw_s, wo_bf.rearrange("p (a b c) -> p a b c", a=NDC, b=NDC, c=128),
                  wo_l.rearrange("a b p c -> p a b c"), after=fence)
        b_wo.setw(wot)

        dn_ap = [S6.ap()[:, 1536:2048], S6.ap()[:, 2048:2560], S6.ap()[:, 2560:3072],
                 C0.ap()[:, 0:512]]
        dn_b = [b_S6[3], b_S6[4], b_S6[5], b_C[0]]
        tp_ap = [S6_bf[:, 3072:4096], S6_bf[:, 4096:5120], S6_bf[:, 5120:6144],
                 C0_bf[:, 0:1024]]

        def loadA(hc):
            sl = hc % 2
            t = DMA("act", wgs_s[sl], wg_sl[sl], wsc[0, hc], after=b_wg[sl].wr() + tuple(fence))
            b_wg[sl].setw(t)
            t = DMA("pool", wus_s[sl], wu_sl[sl], wsc[1, hc], after=b_wu[sl].wr() + tuple(fence))
            b_wu[sl].setw(t)
            t = DMA("sp", wdAs_s[sl], wdA_sl[sl], wsc[2, hc][:, 0:512],
                    after=b_wdA[sl].wr() + tuple(fence))
            b_wdA[sl].setw(t)

        def loadB(hc):
            sl = hc % 2
            t = DMA("sp", wdBs_s[sl], wdB_sl[sl], wsc[2, hc][:, 512:1024],
                    after=b_wdB[sl].wr() + tuple(fence))
            b_wdB[sl].setw(t)

        loadA(0)
        loadA(1)
        loadB(0)
        loadB(1)

        xnt_t2 = [xnt_t, xnT.ap()[:, 28160:32256]]
        b_xnt2 = [b_xnt3, Buf()]
        xp6 = [SCR.ap()[:, 14336 + i * 2048:14336 + (i + 1) * 2048].bitcast(F32)
               for i in range(6)]
        b_xp6 = [Buf() for _ in range(6)]
        lnps = S6_bf[:, 2048:3072]
        b_lnps = b_S6[2]

        def emit_ln(it, rps=(0, 1, 2, 3)):
            dst = xnt_t2[it % 2]
            b_dst = b_xnt2[it % 2]
            dv = dst.rearrange("p (a r c) -> p a r c", a=NDC, r=4, c=128)
            for rp in rps:
                rt = it * 4 + rp
                slot = rt % 6
                xt = DMA("sp", xp_sl[slot], xp6[slot][:, :], xin[rt * 128:(rt + 1) * 128, :],
                         after=b_xp6[slot].wr() + tuple(fence))
                b_xp6[slot].setw(xt)
                at = E("act", lambda sc, r=rt, s=slot: sc.activation(
                    xnb3[:, :], xp6[s][:, :], AF.Identity,
                    bias=lnbi.ap()[:, r:r + 1], scale=lnsc.ap()[:, r:r + 1]),
                    after=b_xp6[slot].rd() + b_ln.rd() + b_xnb3.wr() + tuple(fence),
                    signal=True)
                b_xp6[slot].addr(at)
                b_xnb3.setw(at)
                tt = None
                for dc in range(NDC):
                    tt = E("pe", lambda te, d=dc: te.transpose(
                        lnps[:, d * 128:(d + 1) * 128], xnb3[:, d * 128:(d + 1) * 128],
                        idn.ap()[:, :]),
                        after=(at,) + (b_lnps.wr() if dc == 0 else ()),
                        signal=True)
                b_xnb3.addr(tt)
                b_lnps.setw(tt)
                lv = lnps.rearrange("p (a c) -> p a c", a=NDC, c=128)
                ct2 = E("dve", lambda v, p=rp, lv_=lv, dv_=dv: v.tensor_copy(
                    dv_[:, :, p, :], lv_[:, :, :]),
                    after=(tt,) + b_dst.wr(), signal=True)
                b_lnps.addr(ct2)
                b_dst.addw(ct2)

        def emit_stile(it):
            for oc in range(NDC):
                reg = S6.ap()[:, (oc % 2) * 512:(oc % 2) * 512 + 512]
                bank = b_S6[oc % 2]
                mt = None
                for dc in range(NDC):
                    mt = E("pe", lambda te, o=oc, d=dc, r=reg, i=it: te.matmul(
                        r[:, :], wo_bf[:, o * 1024 + d * 128: o * 1024 + (d + 1) * 128],
                        ctxT.ap()[:, d * S + i * 512: d * S + (i + 1) * 512],
                        start=(d == 0), stop=(d == NDC - 1), skip_group_check=True),
                        after=b_wo.rd() + b_ctxT.rd() + (bank.wr() if dc == 0 else ())
                        + tuple(fence),
                        signal=(dc == NDC - 1))
                bank.setw(mt)
                et = E("act", lambda sc, o=oc, r=reg: sc.activation(
                    s_t[:, o * 512:(o + 1) * 512], r[:, :], AF.Identity,
                    bias=bos.ap()[:, o:o + 1]),
                    after=(mt, cs_tok) + b_st.wr() + tuple(fence), signal=True)
                bank.addr(et)
                b_st.addw(et)

        emit_stile(0)
        emit_ln(0)

        for it in range(nIT):
            xnt_c = xnt_t2[it % 2]
            b_xntc = b_xnt2[it % 2]
            gateb = S6.ap()[:, 0:512]
            upb = S6.ap()[:, 512:1024]
            dn_pend = None

            def emit_downA(job):
                hcj, slj, hpj = job
                for oc in range(4):
                    dt = E("pe", lambda te, h=hcj, o=oc, s=slj: te.matmul(
                        dn_ap[o][:, :], wdA_sl[s][:, o * 128:(o + 1) * 128],
                        hpr[:, h * 512:(h + 1) * 512],
                        start=(h == 0), stop=(h == NHC - 1), skip_group_check=True),
                        after=(hpj,) + b_wdA[slj].rd()
                        + (dn_b[oc].wr() if hcj == 0 else ()),
                        signal=(oc == 3))
                    if hcj == 0 and oc == 3:
                        for o2 in range(4):
                            dn_b[o2].setw(dt)
                    elif hcj == NHC - 1 and oc == 3:
                        for o2 in range(4):
                            dn_b[o2].addw(dt)
                    b_wdA[slj].addr(dt)

            for hc in range(NHC):
                sl = hc % 2
                gt = None
                for dc in range(NDC):
                    gt = E("pe", lambda te, d=dc, s=sl, x_=xnt_c: te.matmul(
                        gateb[:, :], wg_sl[s][:, d * 128:(d + 1) * 128],
                        x_[:, d * 512:(d + 1) * 512],
                        start=(d == 0), stop=(d == NDC - 1), skip_group_check=True),
                        after=b_wg[sl].rd() + b_xntc.rd()
                        + (b_S6[0].wr() if dc == 0 else ()),
                        signal=(dc == NDC - 1))
                b_S6[0].setw(gt)
                b_wg[sl].addr(gt)
                b_xntc.addr(gt)
                ut = None
                for dc in range(NDC):
                    ut = E("pe", lambda te, d=dc, s=sl, x_=xnt_c: te.matmul(
                        upb[:, :], wu_sl[s][:, d * 128:(d + 1) * 128],
                        x_[:, d * 512:(d + 1) * 512],
                        start=(d == 0), stop=(d == NDC - 1), skip_group_check=True),
                        after=b_wu[sl].rd() + b_xntc.rd()
                        + (b_S6[1].wr() if dc == 0 else ()),
                        signal=(dc == NDC - 1))
                b_S6[1].setw(ut)
                b_wu[sl].addr(ut)
                b_xntc.addr(ut)
                sg = E("act", lambda sc: sc.activation(
                    sgb[:, :], gateb[:, :], AF.Silu, scale=1.0 / 127.0),
                    after=(gt,) + b_sgb.wr(), signal=True)
                b_S6[0].addr(sg)
                b_sgb.setw(sg)
                hp = E("dve", lambda v, h=hc: v.tensor_mul(
                    hpr[:, h * 512:(h + 1) * 512], sgb[:, :], upb[:, :]),
                    after=(sg, ut) + b_hpr.wr(), signal=True)
                b_S6[1].addr(hp)
                b_sgb.addr(hp)
                b_hpr.addw(hp)
                if dn_pend is not None:
                    emit_downA(dn_pend)
                    nhc = dn_pend[0] + 2
                    if nhc < NHC:
                        slw = nhc % 2
                        t = DMA("sp", wdAs_s[slw], wdA_sl[slw], wsc[2, nhc][:, 0:512],
                                after=b_wdA[slw].wr() + tuple(fence))
                        b_wdA[slw].setw(t)
                dn_pend = (hc, sl, hp)
                if hc + 2 < NHC:
                    slw = (hc + 2) % 2
                    t = DMA("act", wgs_s[slw], wg_sl[slw], wsc[0, hc + 2],
                            after=b_wg[slw].wr() + tuple(fence))
                    b_wg[slw].setw(t)
                    t = DMA("pool", wus_s[slw], wu_sl[slw], wsc[1, hc + 2],
                            after=b_wu[slw].wr() + tuple(fence))
                    b_wu[slw].setw(t)
                if hc == 12 and it + 1 < nIT:
                    emit_ln(it + 1, rps=(0, 1))
            emit_downA(dn_pend)
            for oc in range(4):
                g2 = E("act", lambda sc, o=oc: sc.activation(
                    gbf[:, o * 512:(o + 1) * 512], dn_ap[o][:, :], AF.Silu,
                    scale=1.0 / (127.0 * 127.0)),
                    after=dn_b[oc].rd() + b_gbf.wr(), signal=True)
                dn_b[oc].addr(g2)
                b_gbf.addw(g2)
                pt = E("dve", lambda v, o=oc: v.tensor_mul(
                    prod[:, o * 512:(o + 1) * 512], s_t[:, o * 512:(o + 1) * 512],
                    gbf[:, o * 512:(o + 1) * 512]),
                    after=(g2,) + b_st.rd() + b_prod.wr(), signal=True)
                b_gbf.addr(pt)
                b_st.addr(pt)
                b_prod.addw(pt)
            for hc in range(NHC):
                sl = hc % 2
                for oc in range(4):
                    dt = E("pe", lambda te, h=hc, o=oc, s=sl: te.matmul(
                        dn_ap[o][:, :], wdB_sl[s][:, o * 128:(o + 1) * 128],
                        hpr[:, h * 512:(h + 1) * 512],
                        start=(h == 0), stop=(h == NHC - 1), skip_group_check=True),
                        after=b_hpr.rd() + b_wdB[sl].rd()
                        + (dn_b[oc].wr() if hc == 0 else ()),
                        signal=(oc == 3))
                    if hc == 0 and oc == 3:
                        for o2 in range(4):
                            dn_b[o2].setw(dt)
                    elif hc == NHC - 1 and oc == 3:
                        for o2 in range(4):
                            dn_b[o2].addw(dt)
                    b_wdB[sl].addr(dt)
                    b_hpr.addr(dt)
                if hc + 2 < NHC:
                    loadB(hc + 2)
                if hc == 0 and it + 1 < nIT:
                    loadA(0)
                if hc == 1 and it + 1 < nIT:
                    loadA(1)
                if hc == NHC - 2 and it + 1 < nIT:
                    loadB(0)
                if hc == NHC - 1 and it + 1 < nIT:
                    loadB(1)
            for oc in range(4):
                oo = oc + 4
                g2 = E("act", lambda sc, o=oc: sc.activation(
                    gbf[:, o * 512:(o + 1) * 512], dn_ap[o][:, :], AF.Silu,
                    scale=1.0 / (127.0 * 127.0)),
                    after=dn_b[oc].rd() + b_gbf.wr(), signal=True)
                dn_b[oc].addr(g2)
                b_gbf.addw(g2)
                pt = E("dve", lambda v, o=oc, o2=oo: v.tensor_mul(
                    prod[:, o2 * 512:(o2 + 1) * 512], s_t[:, o2 * 512:(o2 + 1) * 512],
                    gbf[:, o * 512:(o + 1) * 512]),
                    after=(g2,) + b_st.rd() + b_prod.wr(), signal=True)
                b_gbf.addr(pt)
                b_st.addr(pt)
                b_prod.addw(pt)
            if it + 1 < nIT:
                emit_stile(it + 1)
            for rp in range(4):
                rt = it * 4 + rp
                slot = rt % 6
                tb = tp_ap[rp]
                bb = dn_b[rp]
                tt = None
                for oc in range(NDC):
                    tt = E("pe", lambda te, o=oc, p=rp, t_=tb: te.transpose(
                        t_[:, o * 128:(o + 1) * 128],
                        prod[:, o * 512 + p * 128: o * 512 + p * 128 + 128],
                        idn.ap()[:, :]),
                        after=b_prod.rd() + (bb.wr() if oc == 0 else ()),
                        signal=True)
                bb.setw(tt)
                b_prod.addr(tt)
                ot = E("dve", lambda v, s=slot, t_=tb: v.tensor_add(
                    xp6[s][:, :], xp6[s][:, :], t_[:, :]),
                    after=(tt,) + b_xp6[slot].rd() + b_xp6[slot].wr(), signal=True)
                bb.addr(ot)
                b_xp6[slot].setw(ot)
                st2 = DMA("sp", xp_sl[slot], out_d[rt * 128:(rt + 1) * 128, :],
                          xp6[slot][:, :], after=b_xp6[slot].rd())
                b_xp6[slot].addr(st2)
                if rp == 1 and it + 1 < nIT:
                    emit_ln(it + 1, rps=(2, 3))

        ndbg = 0
        if DEBUG_BUFS:
            reg = {
                "xnT": (xnT.ap()[:, :], [128, max(NDC * S, 32256)], BF16),
                "ctxT": (ctxT.ap()[:, :], [128, max(NDC * S, 8192)], BF16),
                "Qg": (Qg, [128, S], BF16),
                "Kg": (Kg, [128, S], BF16),
                "Vt": (Vt, [128, VT_COLS], BF16),
                "s_t": (s_t, [128, 4096], BF16),
                "hpr": (hpr, [128, 12288], BF16),
                "prod": (prod, [128, 4096], BF16),
                "gbf": (gbf, [128, 2048], BF16),
                "xnt_t": (xnt_t, [128, 4096], BF16),
                "lnsc": (lnsc.ap()[:, :], [128, nRT], F32),
                "lnbi": (lnbi.ap()[:, :], [128, nRT], F32),
            }
            endf = sch.fence() + [(out_s, 16 * nRT)]
            for nm in DEBUG_BUFS:
                if nm not in reg:
                    continue
                ap, shp, dt = reg[nm]
                dd = nc.dram_tensor("dbg_" + nm, shp, dt, kind="ExternalOutput").ap()
                DMA("sp", out_s, dd, ap, after=endf)
                ndbg += 1
        final_after = [(s_, sch.cnt[id(s_)]) for s_ in xp_sl]
        final_after.append((out_s, 16 * (ndbg + ndbg_e[0])))
        E("sp", lambda e: e.nop(), after=final_after)

        with nc.Block() as block:
            sch.replay(block)

    return nc


def _perm():
    idx = np.zeros(D, dtype=np.int64)
    for h in range(NH):
        for r in range(DK):
            idx[h * DK + r] = h * DK + (2 * r if r < 32 else 2 * (r - 32) + 1)
    return idx


def _host_prep(S, Wq, bq, Wk, bk, Wv, bv, Wo, bo, w_gate, w_up, w_down):
    pi = _perm()
    wq = np.ascontiguousarray(np.asarray(Wq, dtype=np.float32)[pi])
    wk = np.ascontiguousarray(np.asarray(Wk, dtype=np.float32)[pi])
    wv = np.asarray(Wv, dtype=np.float32)
    wo = np.asarray(Wo, dtype=np.float32)
    lay = lambda w, n: np.ascontiguousarray(
        w.reshape(n, 128, NDC, 128).transpose(0, 2, 3, 1))
    d = {
        "wq_l": lay(wq, NG), "wk_l": lay(wk, NG), "wv_l": lay(wv, NG),
        "wo_l": lay(wo, NDC),
        "bq_l": np.ascontiguousarray(np.asarray(bq, np.float32)[pi].reshape(NG, 128).T),
        "bk_l": np.ascontiguousarray(np.asarray(bk, np.float32)[pi].reshape(NG, 128).T),
        "bo_l": np.ascontiguousarray(np.asarray(bo, np.float32).reshape(NDC, 128).T),
        "bv_l": np.ascontiguousarray(np.asarray(bv, np.float32).reshape(1, D)),
        "wg_l": np.ascontiguousarray(
            np.asarray(w_gate, np.int32).reshape(NHC, 128, NDC, 128).transpose(0, 2, 3, 1)),
        "wu_l": np.ascontiguousarray(
            np.asarray(w_up, np.int32).reshape(NHC, 128, NDC, 128).transpose(0, 2, 3, 1)),
        "wd_l": np.ascontiguousarray(
            np.asarray(w_down, np.int32).T.reshape(NHC, 128, D)),
    }
    pos = np.arange(S, dtype=np.float32)
    kidx = np.arange(32, dtype=np.float32)
    ang = (pos[:, None] / (10000.0 ** (2.0 * kidx / 64.0))[None, :]).astype(np.float32)
    cp = np.cos(ang).T.astype(np.float32)
    sp = np.sin(ang).T.astype(np.float32)
    c64 = np.concatenate([cp, cp], 0)
    s64 = np.concatenate([-sp, sp], 0)
    d["ctab"] = np.ascontiguousarray(np.concatenate([c64, c64], 0)).astype(ml_dtypes.bfloat16)
    d["stab"] = np.ascontiguousarray(np.concatenate([s64, s64], 0)).astype(ml_dtypes.bfloat16)
    d["tri"] = np.ascontiguousarray(np.triu(np.ones((128, 128), np.float32))).astype(ml_dtypes.bfloat16)
    d["ident"] = np.eye(128, dtype=np.float32).astype(ml_dtypes.bfloat16)
    return d


def kernel(L0_attn_x, L0_attn_Wq, L0_attn_bq, L0_attn_Wk, L0_attn_bk, L0_attn_Wv,
           L0_attn_bv, L0_attn_Wo, L0_attn_bo, L0_ffn_w_gate, L0_ffn_w_up,
           L0_ffn_w_down):
    x = np.asarray(L0_attn_x, dtype=np.float32)
    Bn, S, _ = x.shape
    shared = _host_prep(S, L0_attn_Wq, L0_attn_bq, L0_attn_Wk, L0_attn_bk,
                        L0_attn_Wv, L0_attn_bv, L0_attn_Wo, L0_attn_bo,
                        L0_ffn_w_gate, L0_ffn_w_up, L0_ffn_w_down)
    nc = build_program(S)
    in_maps = [dict(shared, x=np.ascontiguousarray(x[b])) for b in range(Bn)]
    res = run_bass_kernel_spmd(nc, in_maps, core_ids=list(range(Bn)))
    global LAST_DEBUG
    LAST_DEBUG = {nm: np.stack([np.asarray(res.results[b]["dbg_" + nm], dtype=np.float32)
                                for b in range(Bn)], 0)
                  for nm in DEBUG_BUFS}
    return np.stack([res.results[b]["out"] for b in range(Bn)], 0)
